# version 07
# baseline (speedup 1.0000x reference)
import math
import numpy as np
from contextlib import ExitStack

import concourse.bass as bass
import concourse.mybir as mybir
from concourse.bass_utils import run_bass_kernel_spmd

F32 = mybir.dt.float32
BF16 = mybir.dt.bfloat16
I8 = mybir.dt.int8
U8 = mybir.dt.uint8
AF = mybir.ActivationFunctionType
OP = mybir.AluOpType
AX = mybir.AxisListType

PHI75 = 0.674489750196082

def _A(x):
    if x is None or isinstance(x, (int, float)):
        return x
    if hasattr(x, "ap") and not isinstance(x, bass.AP):
        try:
            return x[:]
        except Exception:
            return x.ap()
    return x



def default_cfg():
    return dict(S=4096, D=2048, HPC=4, WIN=32, G=4, NPASS=5, NCORES=8, debug=0, drains=1)


class MK:
    ENGS = ("sync", "gpsimd", "tensor", "scalar", "vector")

    def __init__(self, cfg):
        self.cfg = cfg
        self.S = cfg["S"]
        self.D = cfg["D"]
        self.HPC = cfg["HPC"]
        self.WIN = cfg["WIN"]
        self.G = cfg["G"]
        self.NPASS = cfg["NPASS"]
        self.KK = max(1, self.S // 4)
        self.TT = self.S // 128
        self.DC = self.D // 128
        self.NG = self.TT // self.G
        self.TPC = self.S // 4
        self.TTB = self.TPC // 128
        self.SC = 128.0 ** -0.5
        self.BANDW = 2 * self.WIN + 128
        self.masked = self.S > 2 * self.WIN
        self.nc = bass.Bass("TRN2", target_bir_lowering=False,
                            detect_race_conditions=False)
        self.es = ExitStack()
        self.sems = {}
        self.semn = {}
        self.plan = None

    def sem(self, name):
        if name not in self.sems:
            self.sems[name] = self.es.enter_context(self.nc.semaphore(name))
            self.semn[name] = 0
        return self.sems[name]

    def val(self, name):
        self.sem(name)
        return self.semn[name]

    def E(self, eng, fn, kind="op"):
        self.plan[eng].append({"fn": fn, "kind": kind, "inc": None})

    def wait(self, eng, name, v):
        if v <= 0:
            return
        s = self.sem(name)
        self.E(eng, lambda e, s=s, v=v: e.wait_ge(s, v), kind="wait")

    def inc(self, eng, name, k=1):
        s = self.sem(name)
        self.semn[name] += k
        v = self.semn[name]
        lst = self.plan[eng]
        if lst and lst[-1]["kind"] == "op" and lst[-1]["inc"] is None:
            lst[-1]["inc"] = (s, k)
        elif eng in ("vector", "scalar"):
            self.E(eng, lambda e: e.drain(), kind="other")
            self.E(eng, lambda e, s=s, k=k: e.sem_inc(s, k), kind="sem")
        elif eng == "tensor":
            raise AssertionError(f"unattachable tensor inc for {name}")
        else:
            self.E(eng, lambda e, s=s, k=k: e.sem_inc(s, k), kind="sem")
        return v

    def sbuf(self, name, shape, dt, stack=None):
        return (stack or self.es).enter_context(self.nc.sbuf_tensor(name, shape, dt))

    def psum(self, name, shape, dt=F32):
        return self.es.enter_context(self.nc.psum_tensor(name, shape, dt))

    def barrier(self, tag):
        name = "bar_" + tag
        self.sem(name)
        tgt = self.semn[name] + 5
        self.semn[name] = tgt
        for e in self.ENGS:
            s = self.sems[name]
            lst = self.plan[e]
            if lst and lst[-1]["kind"] == "op" and lst[-1]["inc"] is None:
                lst[-1]["inc"] = (s, 1)
            elif e == "vector":
                self.E(e, lambda eng: eng.drain(), kind="other")
                self.E(e, lambda eng, s=s: eng.sem_inc(s, 1), kind="sem")
            elif e == "scalar":
                self.E(e, lambda eng: eng.drain(), kind="other")
                self.E(e, lambda eng, s=s: eng.sem_inc(s, 1), kind="sem")
            elif e == "tensor":
                self.E(e, lambda eng, s=s, b=self.trps[0], idt=self.identt:
                       eng.transpose(b[:, 0:128], idt[:], idt[:]).then_inc(s, 1),
                       kind="dma")
            else:
                self.E(e, lambda eng, s=s: eng.sem_inc(s, 1), kind="sem")
        for e in self.ENGS:
            s = self.sems[name]
            self.E(e, lambda eng, s=s, tgt=tgt: eng.wait_ge(s, tgt), kind="wait")

    @staticmethod
    def _run_entries(e, lst):
        for ent in lst:
            r = ent["fn"](e)
            if ent["inc"] is not None:
                s, k = ent["inc"]
                r.then_inc(s, k)

    def run_block(self):
        plan = self.plan
        run = self._run_entries
        with self.nc.Block() as block:
            @block.sync
            def _(e, lst=plan["sync"]):
                run(e, lst)

            @block.gpsimd
            def _(e, lst=plan["gpsimd"]):
                run(e, lst)

            @block.tensor
            def _(e, lst=plan["tensor"]):
                run(e, lst)

            @block.scalar
            def _(e, lst=plan["scalar"]):
                run(e, lst)

            @block.vector
            def _(e, lst=plan["vector"]):
                run(e, lst)

    def ts(self, dst, src, s1, s2, op0, op1=None, accum=None, eng="vector"):
        dst, src, s1, s2, accum = _A(dst), _A(src), _A(s1), _A(s2), _A(accum)
        def f(e, dst=dst, src=src, s1=s1, s2=s2, op0=op0, op1=op1, accum=accum):
            if op1 is None:
                return e.tensor_scalar(out=dst, in0=src, scalar1=s1, scalar2=s2,
                                       op0=op0, accum_out=accum)
            return e.tensor_scalar(out=dst, in0=src, scalar1=s1, scalar2=s2,
                                   op0=op0, op1=op1, accum_out=accum)
        self.E(eng, f)

    def tt(self, dst, a, b, op):
        dst, a, b = _A(dst), _A(a), _A(b)
        self.E("vector", lambda e, dst=dst, a=a, b=b, op=op:
               e.tensor_tensor(out=dst, in0=a, in1=b, op=op))

    def stt(self, dst, in0, scalar, in1, op0, op1, accum=None):
        dst, in0, scalar, in1, accum = _A(dst), _A(in0), _A(scalar), _A(in1), _A(accum)
        self.E("vector", lambda e, dst=dst, in0=in0, scalar=scalar, in1=in1,
               op0=op0, op1=op1, accum=accum:
               e.scalar_tensor_tensor(out=dst, in0=in0, scalar=scalar, in1=in1,
                                      op0=op0, op1=op1, accum_out=accum))

    def vdr(self):
        if self.cfg.get("drains"):
            self.E("vector", lambda e: e.drain(), kind="other")

    def sdr(self):
        pass

    def cp(self, dst, src):
        dst, src = _A(dst), _A(src)
        self.E("vector", lambda e, dst=dst, src=src: e.tensor_copy(dst, src))

    def cpred(self, dst, mask, data):
        dst, mask, data = _A(dst), _A(mask), _A(data)
        self.E("vector", lambda e, dst=dst, mask=mask, data=data:
               e.copy_predicated(out=dst, mask=mask, data=data))

    def act(self, dst, src, func, bias=0.0, scale=1.0, accum=None):
        dst, src, bias, accum = _A(dst), _A(src), _A(bias), _A(accum)
        self.E("scalar", lambda e, dst=dst, src=src, func=func, bias=bias,
               scale=scale, accum=accum:
               e.activation(out=dst, in_=src, func=func, bias=bias, scale=scale,
                            accum_out=accum))

    def mm(self, dst, lhsT, rhs, start=True, stop=True):
        dst, lhsT, rhs = _A(dst), _A(lhsT), _A(rhs)
        self.E("tensor", lambda e, dst=dst, lhsT=lhsT, rhs=rhs, start=start, stop=stop:
               e.matmul(dst, lhsT, rhs, start=start, stop=stop))

    def dma(self, eng, dst, src, name, cond=None, slow_ok=False):
        dst, src = _A(dst), _A(src)
        s = self.sem(name)
        self.semn[name] += 16
        if cond is None:
            self.E(eng, lambda e, d=dst, sr=src, s=s, so=slow_ok:
                   e.dma_start(out=d, in_=sr, allow_slow_non_contiguous=so)
                   .then_inc(s, 16), kind="dma")
        else:
            self.E(eng, lambda e, d=dst, sr=src, s=s, c=cond:
                   e.dma_start(out=d, in_=sr, cond=c(e)).then_inc(s, 16), kind="dma")
        return self.semn[name]

    def rred(self, dst, src, op, absval=False):
        dst, src = _A(dst), _A(src)
        self.E("vector", lambda e, dst=dst, src=src, op=op, a=absval:
               e.tensor_reduce(out=dst, in_=src, axis=AX.X, op=op, apply_absolute_value=a))

    def recip(self, dst, src):
        dst, src = _A(dst), _A(src)
        self.E("vector", lambda e, dst=dst, src=src: e.reciprocal(dst, src))

    def max8(self, dst, src):
        dst, src = _A(dst), _A(src)
        self.E("vector", lambda e, dst=dst, src=src: e.max(out=dst, in_=src))

    def mrep(self, buf, top8, immv):
        buf, top8 = _A(buf), _A(top8)
        self.E("vector", lambda e, buf=buf, top8=top8, immv=immv:
               e.match_replace(out=buf, in_to_replace=top8, in_values=buf, imm_value=immv))

    def memset(self, ap_, v):
        ap_ = _A(ap_)
        self.E("vector", lambda e, ap_=ap_, v=v: e.memset(ap_, v))

    def tr_evac(self, srcs, dst, scale=1.0):
        srcs = [_A(s) for s in srcs]
        dst = _A(dst)
        """srcs: list of up to 4 [128 x 128] SBUF APs -> transposed into one
        trps bank -> single scalar evac (with optional scale/dtype cast) to dst
        ([128 x 128*len]). Returns after emitting; sync via t_tra/s_tre."""
        n = len(srcs)
        bank = self.trps[self._trp_par % 2]
        self._trp_par += 1
        self.wait("tensor", "s_tre", self.val("s_tre") - 1)
        for j, sap in enumerate(srcs):
            self.E("tensor", lambda e, b=bank, j=j, sap=sap, idt=self.identt:
                   e.transpose(b[:, j * 128:(j + 1) * 128], sap, idt[:]))
        tv = self.inc("tensor", "t_tra", 1)
        self.wait("scalar", "t_tra", tv)
        self.E("scalar", lambda e, b=bank, dst=dst, n=n, sc=scale:
               e.activation(out=dst, in_=b[:, 0:n * 128], func=AF.Copy, scale=sc))
        ev = self.inc("scalar", "s_tre", 1)
        self._last_ttra = tv
        return ev

    def emit_ln(self, src_ap):
        D = self.D
        St = self.St
        self.rred(self.zc1[:], src_ap, OP.add)
        v1 = self.inc("vector", "v_ln1", 1)
        self.wait("scalar", "v_ln1", v1)
        self.act(self.trash_s[:, 0:D], src_ap, AF.Square, accum=St["tm1"][:, 0:1])
        s1 = self.inc("scalar", "s_ln1", 1)
        self.wait("vector", "s_ln1", s1)
        self.ts(self.zc1[:], self.zc1[:], 1.0 / D, None, OP.mult)
        self.ts(St["tm2"][:, 0:1], St["tm1"][:, 0:1], 1.0 / D, None, OP.mult)
        self.vdr()
        self.tt(St["tm3"][:, 0:1], self.zc1[:], self.zc1[:], OP.mult)
        self.vdr()
        self.tt(St["tm2"][:, 0:1], St["tm2"][:, 0:1], St["tm3"][:, 0:1], OP.subtract)
        v2 = self.inc("vector", "v_ln2", 1)
        self.wait("scalar", "v_ln2", v2)
        self.act(St["tm3"][:, 0:1], St["tm2"][:, 0:1], AF.Sqrt, bias=self.epst[:, 0:1])
        s2 = self.inc("scalar", "s_ln2", 1)
        self.wait("vector", "s_ln2", s2)
        self.recip(St["tm4"][:, 0:1], St["tm3"][:, 0:1])
        self.vdr()
        self.tt(St["tm3"][:, 0:1], self.zc1[:], St["tm4"][:, 0:1], OP.mult)
        self.vdr()
        self.ts(St["tm3"][:, 0:1], St["tm3"][:, 0:1], -1.0, None, OP.mult)
        self.vdr()
        self.ts(self.y2[:], src_ap, St["tm4"][:, 0:1], St["tm3"][:, 0:1], OP.mult, OP.add)
        self.vdr()

    def emit_round(self, q, a, rr, tb, sgn):
        self.stt(a, q, -1.0, q, OP.mult, OP.max)
        qv = self.inc("vector", "v_rnd", 1)
        self.wait("scalar", "v_rnd", qv)
        self.act(sgn, q, AF.Sign)
        sv = self.inc("scalar", "s_rnd", 1)
        self.ts(rr, a, 0.5, None, OP.add)
        self.ts(a, a, 0.5, None, OP.add)
        self.vdr()
        for b in (64.0, 32.0, 16.0, 8.0, 4.0, 2.0, 1.0):
            self.ts(tb, rr, b, -b, OP.is_ge, OP.mult)
            self.tt(rr, rr, tb, OP.add)
            self.vdr()
        self.tt(a, a, rr, OP.subtract)
        self.vdr()
        self.wait("vector", "s_rnd", sv)
        self.tt(q, a, sgn, OP.mult)

    def declare_io(self):
        nc, S, D = self.nc, self.S, self.D
        HSL = self.HPC * 128
        self.xb = nc.dram_tensor("xb", [S, D], F32, kind="ExternalInput").ap()
        self.xres = nc.dram_tensor("xres", [self.TPC, D], F32, kind="ExternalInput").ap()
        self.wq = nc.dram_tensor("wq", [D, HSL], F32, kind="ExternalInput").ap()
        self.wk = nc.dram_tensor("wk", [D, HSL], F32, kind="ExternalInput").ap()
        self.wv = nc.dram_tensor("wv", [D, HSL], F32, kind="ExternalInput").ap()
        self.wod = nc.dram_tensor("wo", [D, D], F32, kind="ExternalInput").ap()
        self.wfd = nc.dram_tensor("wf", [D, D], F32, kind="ExternalInput").ap()
        self.bfd = nc.dram_tensor("bf", [1, D], F32, kind="ExternalInput").ap()
        self.identd = nc.dram_tensor("ident", [128, 128], F32, kind="ExternalInput").ap()
        self.bandd = nc.dram_tensor("bandm", [128, self.BANDW], U8, kind="ExternalInput").ap()
        self.iotad = nc.dram_tensor("iota16", [128, 16], F32, kind="ExternalInput").ap()
        self.outd = nc.dram_tensor("out", [self.TPC, D], F32, kind="ExternalOutput").ap()
        self.ag_in = nc.dram_tensor("ag_in", [HSL, S], F32)
        self.ag_out = nc.dram_tensor("ag_out", [4 * HSL, S], F32)
        self.x1s = nc.dram_tensor("x1s", [self.TPC, D], F32)
        self.wqts = nc.dram_tensor("wqts", [self.DC, self.DC, 128, 128], BF16)
        if self.cfg.get("debug"):
            self.dbg_w = nc.dram_tensor("dbg_w", [self.HPC * self.TT, 128, S], F32,
                                        kind="ExternalOutput").ap()
            self.dbg_fin = nc.dram_tensor("dbg_fin", [self.HPC * self.TT, 128, 16], F32,
                                          kind="ExternalOutput").ap()
            self.dbg_stt = nc.dram_tensor("dbg_stt", [self.HPC * self.TT, 128, 8], F32,
                                          kind="ExternalOutput").ap()

    def alloc_shared(self):
        sb = self.sbuf
        G = self.G
        self.identt = sb("identt", [128, 128], F32)
        self.bandt = sb("bandt", [128, self.BANDW], U8)
        self.iotat = sb("iotat", [128, 16], F32)
        self.epst = sb("epst", [128, 1], F32)
        self.trash_s = sb("trash_s", [128, self.S], I8)
        self.zc1 = sb("zc1", [128, 1], F32)
        self.zcand = sb("zcand", [128, 1], F32)
        self.xsc = sb("xsc", [128, 1], F32)
        self.wTsb = [sb(f"wTsb{i}", [128, 512], F32) for i in range(2)]
        self.oTst = [sb(f"oTst{i}", [128, 128], F32) for i in range(2)]
        self.ws16 = sb("ws16", [128, self.DC], F32)
        self.y2 = sb("y2", [128, self.D], F32)
        names = ("t", "c", "lot", "loc", "hit", "hic", "hvl", "hvh", "mu", "sg",
                 "rsg", "CC", "negC", "th", "nth", "e2", "thhi",
                 "tm1", "tm2", "tm3", "tm4", "dens", "rdens", "tup", "tdn")
        self.StP = [{n: sb(f"st{p}_" + n, [128, G], F32) for n in names}
                    for p in range(3)]
        self.Mi8P = [{n: sb(f"m8{p}_" + n, [128, G], I8) for n in ("a", "b", "c", "d")}
                     for p in range(3)]
        self.St = self.StP[0]
        self.Mi8 = self.Mi8P[0]
        self.scps = self.psum("scps", [128, max(2 * min(1024, self.S), self.D)], F32)
        self.trps = [self.psum(f"trps{i}", [128, 512], F32) for i in range(2)]
        self.pvps = self.psum("pvps", [128, 512], F32)
        self.bps = self.psum("bps", [128, 512], F32)
        self._trp_par = 0
        self.zrel = {}
        self._qk_q = 0
        self._last_ttra = 0
        self._xn_rel = 0
        self._odv_rel = {0: 0, 1: 0}
        self._pv_done = {}
        self._zrec_done = {}

    def alloc_a(self, st):
        sb = self.sbuf
        S, D, DC, TT, G = self.S, self.D, self.DC, self.TT, self.G
        self.xt = [sb(f"xt{i}", [128, D], F32, st) for i in range(2)]
        self.xn = sb("xn", [128, D], F32, st)
        self.lnT = [sb("lnT0", [128, DC, 128], F32, st)] * 2
        self.rstdc = sb("rstdc", [128, TT], F32, st)
        self.negmrc = sb("negmrc", [128, TT], F32, st)
        self.wqh = sb("wqh", [128, DC, 128], F32, st)
        self.wkh = sb("wkh", [128, DC, 128], F32, st)
        self.wvh = sb("wvh", [128, DC, 128], F32, st)
        self.qT = sb("qT", [128, S], F32, st)
        self.kT = sb("kT", [128, S], F32, st)
        self.vT = sb("vT", [128, TT, 128], F32, st)
        self.zt = sb("zt", [128, G, S], F32, st)
        self.trash_v = sb("trash_v", [128, S], I8, st)
        self.ubuf = sb("ubuf", [128, S // 4], F32, st)
        self.cand = sb("cand", [128, 32], F32, st)
        self.fin16 = sb("fin16", [128, 16], F32, st)
        self.zb = sb("zb", [128, self.BANDW], F32, st)
        self.gsb = sb("gsb", [128, 128], F32, st)
        self.kbar = sb("kbar", [128, 1], F32, st)
        self.odv = [sb(f"odv{i}", [128, 128], F32, st) for i in range(2)]
        self.wqbf = [sb(f"wqbf{i}", [128, 128], BF16, st) for i in range(2)]
        self.qtm = sb("qtm", [128, 128], F32, st)
        self.zrec = sb("zrec", [128, G], F32, st)

    def emit_consts(self):
        v = self.dma("sync", self.identt[:], self.identd, "d_const")
        v = self.dma("sync", self.bandt[:], self.bandd, "d_const")
        v = self.dma("sync", self.iotat[:], self.iotad, "d_const")
        for e in ("vector", "scalar", "tensor"):
            self.wait(e, "d_const", v)
        self.memset(self.epst[:], 1e-5)
        vc = self.inc("vector", "v_const", 1)
        self.wait("scalar", "v_const", vc)
        self.wait("tensor", "v_const", vc)
        self.E("scalar", lambda e: e.preload_act_table(AF.Sqrt), kind="other")

    def emit_build_head(self, h):
        DC, TT, D = self.DC, self.TT, self.D
        self.wait("sync", "t_pj", 3 * self.TT * h)
        pvv = self.val("t_pvg")
        self.wait("vector", "t_pvg", pvv)
        self.wait("scalar", "t_pvg", pvv)
        for w_dram, w_sb in ((self.wq, self.wqh), (self.wk, self.wkh), (self.wv, self.wvh)):
            src = w_dram[:, h * 128:(h + 1) * 128].rearrange("(c p) n -> p c n", p=128)
            self.dma("sync", w_sb[:], src, "d_wh")
        self.wait("tensor", "d_wh", self.val("d_wh"))
        for i in range(TT):
            buf = self.xt[i % 2]
            gidx = h * TT + i
            self.wait("sync", "v_nrm", gidx - 1)
            dxn = f"d_x{i % 2}"
            vx = self.dma("sync", buf[:], self.xb[i * 128:(i + 1) * 128, :], dxn)
            self.wait("vector", dxn, vx)
            St = self.St
            if h == 0:
                self.rred(self.zc1[:], buf[:], OP.add)
                v1 = self.inc("vector", "v_bs1", 1)
                self.wait("scalar", "v_bs1", v1)
                self.wait("scalar", dxn, vx)
                self.act(self.trash_s[:, 0:D], buf[:], AF.Square, accum=St["tm1"][:, 0:1])
                s1 = self.inc("scalar", "s_bs1", 1)
                self.wait("vector", "s_bs1", s1)
                self.ts(self.zc1[:], self.zc1[:], 1.0 / D, None, OP.mult)
                self.ts(St["tm2"][:, 0:1], St["tm1"][:, 0:1], 1.0 / D, None, OP.mult)
                self.vdr()
                self.tt(St["tm3"][:, 0:1], self.zc1[:], self.zc1[:], OP.mult)
                self.vdr()
                self.tt(St["tm2"][:, 0:1], St["tm2"][:, 0:1], St["tm3"][:, 0:1], OP.subtract)
                v2 = self.inc("vector", "v_bs2", 1)
                self.wait("scalar", "v_bs2", v2)
                self.act(St["tm3"][:, 0:1], St["tm2"][:, 0:1], AF.Sqrt, bias=self.epst[:, 0:1])
                s2 = self.inc("scalar", "s_bs2", 1)
                self.wait("vector", "s_bs2", s2)
                self.recip(self.rstdc[:, i:i + 1], St["tm3"][:, 0:1])
                self.vdr()
                self.tt(St["tm3"][:, 0:1], self.zc1[:], self.rstdc[:, i:i + 1], OP.mult)
                self.vdr()
                self.ts(self.negmrc[:, i:i + 1], St["tm3"][:, 0:1], -1.0, None, OP.mult)
                self.vdr()
            self.wait("vector", "t_tra", self._xn_rel)
            self.ts(self.xn[:], buf[:], self.rstdc[:, i:i + 1], self.negmrc[:, i:i + 1],
                    OP.mult, OP.add)
            vn = self.inc("vector", "v_nrm", 1)
            self.wait("tensor", "v_nrm", vn)
            lnb = self.lnT[i % 2]
            self.wait("scalar", "t_pj", 3 * gidx)
            for g4 in range(DC // 4):
                self.tr_evac([self.xn[:, (g4 * 4 + j) * 128:(g4 * 4 + j + 1) * 128]
                              for j in range(4)],
                             lnb[:, g4 * 4:(g4 + 1) * 4, :])
            self._xn_rel = self.val("t_tra")
            self.wait("tensor", "s_tre", self.val("s_tre"))
            self.wait("tensor", "s_pe", 3 * gidx)
            for c in range(DC):
                self.mm(self.scps[:, 0:128], self.wqh[:, c, :], lnb[:, c, :],
                        start=(c == 0), stop=(c == DC - 1))
            self.inc("tensor", "t_pj", 1)
            for c in range(DC):
                self.mm(self.scps[:, 512:640], self.wkh[:, c, :], lnb[:, c, :],
                        start=(c == 0), stop=(c == DC - 1))
            self.inc("tensor", "t_pj", 1)
            for c in range(DC):
                self.mm(self.scps[:, 1024:1152], lnb[:, c, :], self.wvh[:, c, :],
                        start=(c == 0), stop=(c == DC - 1))
            tpj = self.inc("tensor", "t_pj", 1)
            self.wait("scalar", "t_pj", tpj - 2)
            self.E("scalar", lambda e, i=i: e.activation(
                out=self.qT[:, i * 128:(i + 1) * 128], in_=self.scps[:, 0:128], func=AF.Copy))
            self.inc("scalar", "s_pe", 1)
            self.wait("scalar", "t_pj", tpj - 1)
            self.E("scalar", lambda e, i=i: e.activation(
                out=self.kT[:, i * 128:(i + 1) * 128], in_=self.scps[:, 512:640], func=AF.Copy))
            self.inc("scalar", "s_pe", 1)
            self.wait("scalar", "t_pj", tpj)
            self.E("scalar", lambda e, i=i: e.activation(
                out=self.vT[:, i, 0:128], in_=self.scps[:, 1024:1152], func=AF.Copy))
            self.inc("scalar", "s_pe", 1)

    def emit_head_stats(self, h):
        TT = self.TT
        self.wait("vector", "s_pe", self.val("s_pe"))
        self.wait("vector", "t_st", self.val("t_st"))
        self.rred(self.kbar[:], self.kT[:], OP.add)
        kv = self.inc("vector", "v_kb", 1)
        self.wait("tensor", "v_kb", kv)
        for i in range(TT):
            dst = self.wTsb[i % 2][:, 0:128]
            self.wait("scalar", "t_gk", self.val("t_gk") - 1)
            ev = self.tr_evac([self.kT[:, i * 128:(i + 1) * 128]], dst)
            self.wait("tensor", "s_tre", ev)
            self.mm(self.bps[:, 0:128], dst, dst, start=(i == 0), stop=(i == TT - 1))
            self.inc("tensor", "t_gk", 1)
        self.wait("scalar", "t_gk", self.val("t_gk"))
        self.E("scalar", lambda e, sc=self.SC * self.SC / self.S:
               e.activation(out=self.gsb[:], in_=self.bps[:, 0:128], func=AF.Copy, scale=sc))
        self.inc("scalar", "s_g", 1)

    def emit_group_stats(self, h, g):
        St, G, S = self.St, self.G, self.S
        self.wait("vector", "s_exp", self.val("s_exp"))
        self.wait("vector", "s_cnt", self.val("s_cnt"))
        self.wait("vector", "s_thh", self.val("s_thh"))
        self.wait("tensor", "s_g", self.val("s_g"))
        for gi in range(G):
            i = g * G + gi
            qc = self.qT[:, i * 128:(i + 1) * 128]
            self.wait("tensor", "s_st", self.val("s_st"))
            self.wait("tensor", "v_st", self.val("v_st"))
            self.mm(self.bps[:, 384 + gi:385 + gi], qc, self.kbar[:])
            self.mm(self.bps[:, 0:128], qc, self.gsb[:])
            self.wait("scalar", "v_st", self.val("v_st"))
            ev = self.tr_evac([qc], self.qtm[:])
            tv = self._last_ttra
            self.wait("scalar", "t_tra", tv)
            self.E("scalar", lambda e, gi=gi, sc=self.SC / self.S:
                   e.activation(out=St["mu"][:, gi:gi + 1],
                                in_=self.bps[:, 384 + gi:385 + gi], func=AF.Copy, scale=sc))
            sv = self.inc("scalar", "s_st", 1)
            self.wait("vector", "s_st", sv)
            self.wait("vector", "s_tre", ev)
            self.wait("vector", "t_tra", tv)
            self.stt(self.qtm[:], self.bps[:, 0:128], 1.0, self.qtm[:], OP.mult, OP.mult,
                     accum=St["e2"][:, gi:gi + 1])
            self.inc("vector", "v_st", 1)
            self.vdr()
        self.tt(St["tm1"], St["mu"], St["mu"], OP.mult)
        self.vdr()
        self.tt(St["tm2"], St["e2"], St["tm1"], OP.subtract)
        self.vdr()
        self.ts(St["tm2"], St["tm2"], 1e-12, None, OP.max)
        self.vdr()
        self.ts(St["tm3"], St["tm2"], 1.0, None, OP.add)
        self.vdr()
        self.recip(St["tm4"], St["tm3"])
        self.vdr()
        self.ts(St["tm4"], St["tm4"], 2.0, None, OP.mult)
        self.vdr()
        for _ in range(4):
            self.tt(St["tm3"], St["tm4"], St["tm4"], OP.mult)
            self.vdr()
            self.tt(St["tm3"], St["tm3"], St["tm2"], OP.mult)
            self.vdr()
            self.ts(St["tm3"], St["tm3"], -0.5, 1.5, OP.mult, OP.add)
            self.vdr()
            self.tt(St["tm4"], St["tm4"], St["tm3"], OP.mult)
            self.vdr()
        self.cp(St["rsg"], St["tm4"])
        self.tt(St["sg"], St["tm2"], St["tm4"], OP.mult)
        self.vdr()
        self.ts(St["tm1"], St["sg"], 6.0, None, OP.mult)
        self.vdr()
        self.tt(St["CC"], St["mu"], St["tm1"], OP.add)
        self.vdr()
        self.ts(St["negC"], St["CC"], -1.0, None, OP.mult)
        self.ts(St["tm1"], St["sg"], PHI75, None, OP.mult)
        self.vdr()
        self.tt(St["t"], St["mu"], St["tm1"], OP.add)
        self.vdr()
        self.tt(St["tm1"], St["t"], St["CC"], OP.subtract)
        vv = self.inc("vector", "v_th", 1)
        self.wait("scalar", "v_th", vv)
        self.act(St["th"], St["tm1"], AF.Exp)
        sv = self.inc("scalar", "s_thx", 1)
        self.wait("vector", "s_thx", sv)
        self.ts(St["nth"], St["th"], -1.0, None, OP.mult)
        for nm, v in (("lot", 0.0), ("loc", float(S)), ("hit", 0.0), ("hic", 0.0),
                      ("hvl", 0.0), ("hvh", 0.0)):
            self.memset(St[nm][:], v)
        self.inc("vector", "v_gini", 1)
        self.vdr()

    def emit_qk_slot(self, h, g, gi):
        G, S = self.G, self.S
        nC = self.St["negC"]
        QW = min(1024, S)
        nq = S // QW
        i = g * G + gi
        qc = self.qT[:, i * 128:(i + 1) * 128]
        prev_tile = h * self.TT + i - G
        if prev_tile in self.zrel:
            self.wait("scalar", "t_tra", self.zrel[prev_tile])
        self.wait("scalar", "v_gini", self.val("v_gini"))
        for qq in range(nq):
            reg = QW * (self._qk_q % 2)
            self._qk_q += 1
            self.wait("tensor", "s_exp", self.val("s_exp") - 1)
            for cc in range(QW // 512):
                k0 = qq * QW + cc * 512
                self.mm(self.scps[:, reg + cc * 512: reg + (cc + 1) * 512],
                        qc, self.kT[:, k0:k0 + 512])
            tv = self.inc("tensor", "t_qk", 1)
            self.wait("scalar", "t_qk", tv)
            self.E("scalar", lambda e, gi=gi, qq=qq, reg=reg, QW=QW, sc=self.SC, nC=nC:
                   e.activation(out=self.zt[:, gi, qq * QW:(qq + 1) * QW],
                                in_=self.scps[:, reg:reg + QW], func=AF.Exp,
                                bias=nC[:, gi:gi + 1], scale=sc))
            self.inc("scalar", "s_exp", 1)

    def emit_group_counts(self, h, g):
        St, G, S = self.St, self.G, self.S
        half = G // 2
        for it in range(self.NPASS):
            self.wait("scalar", "v_nthx", self.val("v_nthx"))
            self.wait("scalar", "s_exp", self.val("s_exp"))
            for gi in range(half):
                self.act(self.trash_s[:], self.zt[:, gi, :], AF.Sign,
                         bias=St["nth"][:, gi:gi + 1], accum=St["c"][:, gi:gi + 1])
            cv = self.inc("scalar", "s_cnt", 1)
            self.wait("vector", "s_exp", self.val("s_exp"))
            self.wait("vector", "s_thx", self.val("s_thx"))
            for gi in range(half, G):
                self.ts(self.trash_v[:], self.zt[:, gi, :], St["th"][:, gi:gi + 1],
                        None, OP.is_gt, OP.add, accum=St["c"][:, gi:gi + 1])
            self.vdr()
            self.wait("vector", "s_cnt", cv)
            self.ts(St["c"][:, 0:half], St["c"][:, 0:half], 0.5, S / 2.0, OP.mult, OP.add)
            self.vdr()
            self.emit_update(last=(it == self.NPASS - 1))

    def emit_update(self, last):
        St, M = self.St, self.Mi8
        KK, S = self.KK, self.S
        T_AIM = KK - 4.5
        self.ts(St["tm1"], St["c"], KK - 0.5, None, OP.is_le)
        self.vdr()
        self.cp(M["a"][:], St["tm1"])
        self.vdr()
        self.cpred(St["hit"], M["a"][:], St["t"])
        self.cpred(St["hic"], M["a"][:], St["c"])
        self.tt(St["hvh"], St["hvh"], St["tm1"], OP.max)
        self.vdr()
        self.ts(St["tm2"], St["tm1"], -1.0, 1.0, OP.mult, OP.add)
        self.vdr()
        self.cp(M["b"][:], St["tm2"])
        self.vdr()
        self.cpred(St["lot"], M["b"][:], St["t"])
        self.cpred(St["loc"], M["b"][:], St["c"])
        self.tt(St["hvl"], St["hvl"], St["tm2"], OP.max)
        self.vdr()
        if last:
            self.tt(St["tm1"], St["hit"], St["CC"], OP.subtract)
            vv = self.inc("vector", "v_thh", 1)
            self.wait("scalar", "v_thh", vv)
            self.act(St["thhi"], St["tm1"], AF.Exp)
            self.inc("scalar", "s_thh", 1)
            return
        self.tt(St["tm1"], St["t"], St["mu"], OP.subtract)
        self.vdr()
        self.tt(St["tm1"], St["tm1"], St["rsg"], OP.mult)
        vv = self.inc("vector", "v_dz", 1)
        self.wait("scalar", "v_dz", vv)
        self.act(St["tm2"], St["tm1"], AF.Square)
        self.sdr()
        self.act(St["tm2"], St["tm2"], AF.Exp, scale=-0.5)
        sv = self.inc("scalar", "s_dz", 1)
        self.wait("vector", "s_dz", sv)
        self.ts(St["tm2"], St["tm2"], S / math.sqrt(2 * math.pi), None, OP.mult)
        self.vdr()
        self.tt(St["dens"], St["tm2"], St["rsg"], OP.mult)
        self.vdr()
        self.ts(St["dens"], St["dens"], 8.0, None, OP.max)
        self.vdr()
        self.recip(St["rdens"], St["dens"])
        self.vdr()
        self.ts(St["tm1"], St["loc"], KK - 12.0, None, OP.subtract)
        self.vdr()
        self.tt(St["tm1"], St["tm1"], St["rdens"], OP.mult)
        self.vdr()
        self.tt(St["tup"], St["lot"], St["tm1"], OP.add)
        self.ts(St["tm2"], St["hic"], -1.0, T_AIM, OP.mult, OP.add)
        self.vdr()
        self.tt(St["tm2"], St["tm2"], St["rdens"], OP.mult)
        self.vdr()
        self.tt(St["tdn"], St["hit"], St["tm2"], OP.subtract)
        self.vdr()
        self.tt(St["tm1"], St["loc"], St["hic"], OP.subtract)
        self.vdr()
        self.recip(St["tm2"], St["tm1"])
        self.vdr()
        self.ts(St["tm1"], St["loc"], T_AIM, None, OP.subtract)
        self.vdr()
        self.tt(St["tm1"], St["tm1"], St["tm2"], OP.mult)
        self.vdr()
        self.ts(St["tm1"], St["tm1"], 0.05, 0.95, OP.max, OP.min)
        self.vdr()
        self.tt(St["tm3"], St["hit"], St["lot"], OP.subtract)
        self.vdr()
        self.tt(St["tm1"], St["tm1"], St["tm3"], OP.mult)
        self.vdr()
        self.tt(St["tm1"], St["lot"], St["tm1"], OP.add)
        self.vdr()
        self.cp(M["b"][:], St["hvl"])
        self.tt(St["tm2"], St["hvl"], St["hvh"], OP.mult)
        self.vdr()
        self.cp(M["c"][:], St["tm2"])
        self.ts(St["tm3"], St["hic"], KK - 8.5, None, OP.is_ge)
        self.vdr()
        self.tt(St["tm3"], St["tm3"], St["hvh"], OP.mult)
        self.vdr()
        self.cp(M["d"][:], St["tm3"])
        self.vdr()
        self.cp(St["t"], St["tdn"])
        self.vdr()
        self.cpred(St["t"], M["b"][:], St["tup"])
        self.vdr()
        self.cpred(St["t"], M["c"][:], St["tm1"])
        self.vdr()
        self.cpred(St["t"], M["d"][:], St["hit"])
        self.vdr()
        self.tt(St["tm1"], St["t"], St["CC"], OP.subtract)
        vv = self.inc("vector", "v_th2", 1)
        self.wait("scalar", "v_th2", vv)
        self.act(St["th"], St["tm1"], AF.Exp)
        sv = self.inc("scalar", "s_thx", 1)
        self.wait("vector", "s_thx", sv)
        self.ts(St["nth"], St["th"], -1.0, None, OP.mult)
        self.inc("vector", "v_nthx", 1)
        self.vdr()

    def band_slices(self, i):
        W, S = self.WIN, self.S
        k0 = i * 128 - W
        k1 = i * 128 + 128 + W
        b0, b1 = max(0, k0), min(S, k1)
        m0 = b0 - k0
        return b0, b1, m0, m0 + (b1 - b0)

    def emit_tile_finish(self, h, g, gi):
        St, S, KK = self.St, self.S, self.KK
        i = g * self.G + gi
        zrow = self.zt[:, gi, :]
        QW = S // 4
        self.wait("vector", "s_thh", self.val("s_thh"))
        self.ts(self.trash_v[:], zrow, St["thhi"][:, gi:gi + 1], None,
                OP.is_gt, OP.add, accum=St["c"][:, gi:gi + 1])
        self.vdr()
        self.ts(self.zc1[:], St["c"][:, gi:gi + 1], -1.0, float(KK - 1), OP.mult, OP.add)
        self.vdr()
        self.ts(self.zc1[:], self.zc1[:], 0.0, 15.0, OP.max, OP.min)
        self.vdr()
        if self.masked:
            for qq in range(4):
                zq = self.zt[:, gi, qq * QW:(qq + 1) * QW]
                self.stt(self.ubuf[:], zq, St["thhi"][:, gi:gi + 1], zq, OP.is_le, OP.mult)
                self.max8(self.cand[:, qq * 8:(qq + 1) * 8], self.ubuf[:])
            self.vdr()
            self.max8(self.fin16[:, 0:8], self.cand[:])
            self.vdr()
            self.mrep(self.cand[:], self.fin16[:, 0:8], -1.0)
            self.vdr()
            self.max8(self.fin16[:, 8:16], self.cand[:])
            self.vdr()
            self.ts(self.cand[:, 0:16], self.iotat[:], self.zc1[:, 0:1], None, OP.subtract)
            self.vdr()
            self.tt(self.cand[:, 0:16], self.cand[:, 0:16], self.cand[:, 0:16], OP.mult)
            self.vdr()
            self.ts(self.cand[:, 0:16], self.cand[:, 0:16], 0.0625, None, OP.is_le)
            self.vdr()
            self.tt(self.cand[:, 0:16], self.cand[:, 0:16], self.fin16[:], OP.mult)
            self.vdr()
            self.rred(self.zcand[:], self.cand[:, 0:16], OP.add)
            self.vdr()
            b0, b1, m0, m1 = self.band_slices(i)
            bw = b1 - b0
            self.cp(self.zb[:, 0:bw], self.zt[:, gi, b0:b1])
            self.vdr()
            self.stt(zrow, zrow, self.zcand[:, 0:1], zrow, OP.is_ge, OP.mult,
                     accum=St["tm1"][:, gi:gi + 1])
            self.vdr()
            self.rred(St["tm2"][:, gi:gi + 1], self.zt[:, gi, b0:b1], OP.add)
            self.vdr()
            self.cpred(self.zt[:, gi, b0:b1], self.bandt[:, m0:m1], self.zb[:, 0:bw])
            self.vdr()
            self.rred(St["tm3"][:, gi:gi + 1], self.zt[:, gi, b0:b1], OP.add)
            self.vdr()
            self.tt(St["tm1"][:, gi:gi + 1], St["tm1"][:, gi:gi + 1],
                    St["tm2"][:, gi:gi + 1], OP.subtract)
            self.vdr()
            self.tt(St["tm1"][:, gi:gi + 1], St["tm1"][:, gi:gi + 1],
                    St["tm3"][:, gi:gi + 1], OP.add)
            self.vdr()
            self.recip(self.zrec[:, gi:gi + 1], St["tm1"][:, gi:gi + 1])
        else:
            self.rred(St["tm1"][:, gi:gi + 1], zrow, OP.add)
            self.vdr()
            self.recip(self.zrec[:, gi:gi + 1], St["tm1"][:, gi:gi + 1])
        self._zrec_done[h * self.TT + i] = self.inc("vector", "v_zr", 1)
        if self.cfg.get("debug"):
            tg = h * self.TT + i
            fv = self.inc("vector", "v_fin", 1)
            self.wait("sync", "v_fin", fv)
            self.dma("sync", self.dbg_w[tg], self.zt[:, gi, :], "d_dbg")
            self.dma("sync", self.dbg_fin[tg], self.fin16[:], "d_dbg")
            for jj, ap_ in ((0, St["c"][:, gi:gi + 1]), (1, St["thhi"][:, gi:gi + 1]),
                            (2, self.zcand[:]), (3, self.zc1[:]),
                            (4, St["hit"][:, gi:gi + 1]), (5, St["hic"][:, gi:gi + 1])):
                self.dma("sync", self.dbg_stt[tg, :, jj:jj + 1], ap_, "d_dbg", slow_ok=True)
            return
        self.inc("vector", "v_fin", 1)

    def emit_tile_pv_mms(self, h, g, gi):
        TT = self.TT
        i = g * self.G + gi
        tile_glob = h * TT + i
        self.wait("tensor", "v_fin", self.val("v_fin"))
        if self.cfg.get("debug"):
            self.wait("vector", "d_dbg", self.val("d_dbg"))
        pvoff = 128 * (tile_glob % 4)
        self.wait("tensor", "s_pve", self.val("s_pve"))
        ngrp = (TT + 3) // 4
        if self.D >= 2048:
            stg = [self.y2[:, j * 512:(j + 1) * 512] for j in range(4)]
        else:
            stg = [self.wTsb[0][:], self.wTsb[1][:]]
        NS = len(stg)
        V0 = self.val("t_pvg")
        evs = {}

        def emit_T(j):
            ks = [j * 4 + kk for kk in range(4) if j * 4 + kk < TT]
            self.wait("scalar", "t_pvg", V0 + j - NS + 1)
            evs[j] = self.tr_evac([self.zt[:, gi, k * 128:(k + 1) * 128] for k in ks],
                                  stg[j % NS][:, 0:len(ks) * 128])

        emit_T(0)
        for grp in range(ngrp):
            if grp + 1 < ngrp:
                emit_T(grp + 1)
            ks = [grp * 4 + kk for kk in range(4) if grp * 4 + kk < TT]
            self.wait("tensor", "s_tre", evs[grp])
            for jj, k in enumerate(ks):
                self.mm(self.pvps[:, pvoff:pvoff + 128],
                        stg[grp % NS][:, jj * 128:(jj + 1) * 128],
                        self.vT[:, k, 0:128],
                        start=(grp == 0 and jj == 0),
                        stop=(grp == ngrp - 1 and jj == len(ks) - 1))
            self.inc("tensor", "t_pvg", 1)
        self.zrel[tile_glob] = self.val("t_tra")
        self._pv_done[tile_glob] = self.val("t_pvg")

    def emit_tile_out(self, h, g, gi):
        TT = self.TT
        i = g * self.G + gi
        tile_glob = h * TT + i
        pvoff = 128 * (tile_glob % 4)
        par = tile_glob % 2
        self.wait("scalar", "t_pvg", self._pv_done[tile_glob])
        self.wait("scalar", "v_zr", self._zrec_done[tile_glob])
        agn = f"d_ag{par}"
        self.wait("scalar", agn, self.val(agn))
        self.wait("scalar", "t_tra", self._odv_rel[par])
        self.E("scalar", lambda e, pvoff=pvoff, gi=gi, dst=self.odv[par]:
               e.activation(out=dst[:], in_=self.pvps[:, pvoff:pvoff + 128],
                            func=AF.Copy, scale=self.zrec[:, gi:gi + 1]))
        sv = self.inc("scalar", "s_pve", 1)
        self.wait("tensor", "s_pve", sv)
        ev = self.tr_evac([self.odv[par][:]], self.oTst[par][:])
        self._odv_rel[par] = self.val("t_tra")
        self.wait("scalar", "s_tre", ev)
        s = self.sem(agn)
        self.semn[agn] += 16
        self.E("scalar", lambda e, src=self.oTst[par], h=h, i=i, s=s:
               e.dma_start(out=self.ag_in.ap()[h * 128:(h + 1) * 128, i * 128:(i + 1) * 128],
                           in_=src[:]).then_inc(s, 16))

    def emit_wquant(self, h):
        DC, D = self.DC, self.D
        per = (DC + self.HPC - 1) // self.HPC
        tiles = range(h * per, min(DC, (h + 1) * per))
        wft = self.zt[:, 0, 0:D]
        aq = self.zt[:, 1, 0:D]
        rres = self.zt[:, 2, 0:D]
        if 2 * D <= self.S:
            sg_t = self.zt[:, 1, D:2 * D]
            tb = self.zt[:, 0, D:2 * D]
        else:
            raise AssertionError("wquant scratch layout needs 2*D <= S")
        rel = self.val("t_tra")
        self.wait("sync", "t_tra", rel)
        self.wait("vector", "t_tra", rel)
        for ti in tiles:
            self.wait("sync", "v_wq", self.val("v_wq"))
            self.wait("sync", "t_tra", self.val("t_tra"))
            vd = self.dma("sync", wft, self.wfd[ti * 128:(ti + 1) * 128, :], "d_wf")
            self.wait("vector", "d_wf", vd)
            self.rred(self.zc1[:], wft, OP.max, absval=True)
            self.vdr()
            self.ts(self.zc1[:], self.zc1[:], 1.0 / 127.0, 1e-8 / 127.0, OP.mult, OP.max)
            self.vdr()
            self.cp(self.ws16[:, ti:ti + 1], self.zc1[:])
            self.recip(self.zc1[:], self.zc1[:])
            self.vdr()
            self.ts(wft, wft, self.zc1[:, 0:1], None, OP.mult)
            self.vdr()
            self.emit_round(wft, aq, rres, tb, sg_t)
            wv = self.inc("vector", "v_wq", 1)
            self.wait("tensor", "v_wq", wv)
            for c in range(DC):
                qn = f"d_wqt{c % 2}"
                self.wait("scalar", qn, self.val(qn))
                ev = self.tr_evac([wft[:, c * 128:(c + 1) * 128]], self.wqbf[c % 2][:])
                self.wait("scalar", "s_tre", ev)
                s = self.sem(qn)
                self.semn[qn] += 16
                self.E("scalar", lambda e, c=c, t=ti, s=s, buf=self.wqbf[c % 2]:
                       e.dma_start(out=self.wqts.ap()[c, t, :, :], in_=buf[:]).then_inc(s, 16))

    def use_state(self, p):
        self.St = self.StP[p % 3]
        self.Mi8 = self.Mi8P[p % 3]

    def emit_phase_a(self):
        self.plan = {e: [] for e in self.ENGS}
        with ExitStack() as stA:
            self.alloc_a(stA)
            self.emit_consts()
            for h in range(self.HPC):
                self.emit_build_head(h)
                self.emit_head_stats(h)
                self.emit_wquant(h)
                self.E("scalar", lambda e: e.preload_act_table(AF.Exp), kind="other")
                par0 = h * self.NG
                self.use_state(par0)
                self.emit_group_stats(h, 0)
                if self.NG > 1:
                    self.use_state(par0 + 1)
                    self.emit_group_stats(h, 1)
                self.use_state(par0)
                for gi in range(self.G):
                    self.emit_qk_slot(h, 0, gi)
                for g in range(self.NG):
                    self.use_state(par0 + g)
                    self.emit_group_counts(h, g)
                    half = self.G // 2
                    for gi in range(self.G):
                        self.use_state(par0 + g)
                        self.emit_tile_finish(h, g, gi)
                        self.emit_tile_pv_mms(h, g, gi)
                        self.emit_tile_out(h, g, gi)
                        if g + 1 < self.NG and gi % half == half - 1:
                            self.use_state(par0 + g + 1)
                            for gj in range(gi - half + 1, gi + 1):
                                self.emit_qk_slot(h, g + 1, gj)
                    if g + 2 < self.NG:
                        self.use_state(par0 + g + 2)
                        self.emit_group_stats(h, g + 2)
                self.E("scalar", lambda e: e.preload_act_table(AF.Sqrt), kind="other")
            self.barrier("a")
            self.run_block()

    def emit_phase_b(self):
        self.plan = {e: [] for e in self.ENGS}
        D, DC, S = self.D, self.DC, self.S
        with ExitStack() as stB:
            wosb = self.sbuf("wosb", [128, DC, D], F32, stB)
            agt = self.sbuf("agt", [128, DC, 512], F32, stB)
            xrs = self.sbuf("xrs", [128, D], F32, stB)
            self.wait("gpsimd", "d_ag0", self.val("d_ag0"))
            self.wait("gpsimd", "d_ag1", self.val("d_ag1"))
            s = self.sem("cc1")
            self.semn["cc1"] += 1
            self.E("gpsimd", lambda e, s=s:
                   e.collective_compute("AllGather", OP.bypass,
                                        replica_groups=[[0, 1, 2, 3], [4, 5, 6, 7]],
                                        ins=[self.ag_in.ap()], outs=[self.ag_out.ap()])
                   .then_inc(s, 1), kind="dma")
            for c in range(DC):
                self.dma("sync", wosb[:, c, :], self.wod[c * 128:(c + 1) * 128, :], "d_wo")
            self.wait("sync", "cc1", self.val("cc1"))
            condbox = {}

            def getcond(e, j):
                if "hc" not in condbox:
                    condbox["hc"] = e.snap(e.partition_id() % 4)
                return condbox["hc"] == j

            TTB = self.TTB
            nhalf = max(1, (self.TPC + 511) // 512)
            tiles_per_half = max(1, 512 // 128)
            for half in range(nhalf):
                w = min(512, self.TPC - half * 512)
                self.wait("sync", "t_wo", self.val("t_wo"))
                ag3 = self.ag_out.ap().rearrange("(c p) n -> p c n", p=128)
                for j in range(4):
                    src = ag3[:, :, j * self.TPC + half * 512:
                              j * self.TPC + half * 512 + w]
                    self.dma("sync", agt[:, :, 0:w], src, "d_agt",
                             cond=(lambda e, j=j: getcond(e, j)))
                va = self.val("d_agt")
                self.wait("tensor", "d_agt", va)
                self.wait("tensor", "d_wo", self.val("d_wo"))
                for tl in range(tiles_per_half):
                    if half * 512 + tl * 128 >= self.TPC:
                        break
                    tt_ = half * tiles_per_half + tl
                    if tt_ >= 1:
                        self.wait("tensor", "v_x1", self.val("v_x1"))
                    for c in range(DC):
                        for oq in range(D // 512):
                            self.mm(self.scps[:, oq * 512:(oq + 1) * 512],
                                    agt[:, c, tl * 128:(tl + 1) * 128],
                                    wosb[:, c, oq * 512:(oq + 1) * 512],
                                    start=(c == 0), stop=(c == DC - 1))
                    tv = self.inc("tensor", "t_wo", 1)
                    self.wait("sync", "v_x1", self.val("v_x1"))
                    xv = self.dma("sync", xrs[:],
                                  self.xres[tt_ * 128:(tt_ + 1) * 128, :], "d_xr")
                    self.wait("vector", "t_wo", tv)
                    self.wait("vector", "d_xr", xv)
                    self.wait("vector", "d_x1", self.val("d_x1"))
                    self.wait("scalar", "t_wo", tv)
                    self.emit_ln(self.scps[:, 0:D])
                    self.tt(self.y2[:], self.y2[:], xrs[:], OP.add)
                    x1v = self.inc("vector", "v_x1", 1)
                    self.vdr()
                    self.wait("sync", "v_x1", x1v)
                    self.dma("sync", self.x1s.ap()[tt_ * 128:(tt_ + 1) * 128, :],
                             self.y2[:], "d_x1")
            self.wait("sync", "d_x1", self.val("d_x1"))
            self.barrier("b")
            self.run_block()

    def emit_phase_c(self):
        self.plan = {e: [] for e in self.ENGS}
        D, DC = self.D, self.DC
        with ExitStack() as stC:
            wqtsb = self.sbuf("wqtsb", [128, DC, D], BF16, stC)
            wst = self.sbuf("wst", [128, D], F32, stC)
            btt = self.sbuf("btt", [128, D], F32, stC)
            x1t = self.sbuf("x1t", [128, D], F32, stC)
            q32 = self.sbuf("q32", [128, D], F32, stC)
            aq2 = self.sbuf("aq2", [128, D], F32, stC)
            rr2 = self.sbuf("rr2", [128, D], F32, stC)
            sgn2 = self.sbuf("sgn2", [128, D], F32, stC)
            tb2 = self.sbuf("tb2", [128, D], F32, stC)
            xqbf = self.sbuf("xqbf", [128, DC, 128], BF16, stC)
            onesr = self.sbuf("onesr", [1, 128], F32, stC)
            wsrow = self.sbuf("wsrow", [1, D], F32, stC)
            bfrow = self.sbuf("bfrow", [1, D], F32, stC)
            self.E("scalar", lambda e: e.preload_act_table(AF.Sqrt), kind="other")
            self.wait("sync", "d_wqt0", self.val("d_wqt0"))
            self.wait("sync", "d_wqt1", self.val("d_wqt1"))
            for c in range(DC):
                src = self.wqts.ap()[c].rearrange("o p n -> p o n")
                self.dma("sync", wqtsb[:, c, :].rearrange("p (o n) -> p o n", n=128),
                         src, "d_wqL")
            self.memset(onesr[:], 1.0)
            ov = self.inc("vector", "v_one", 1)
            bank = self.trps[0]
            self.wait("tensor", "s_tre", self.val("s_tre"))
            self.E("tensor", lambda e, b=bank, idt=self.identt:
                   e.transpose(b[0:DC, 0:128], self.ws16[:], idt[:]))
            tv = self.inc("tensor", "t_tra", 1)
            self.wait("scalar", "t_tra", tv)
            self.E("scalar", lambda e, b=bank, DC=DC:
                   e.activation(out=self.wTsb[0][0:DC, 0:128], in_=b[0:DC, 0:128],
                                func=AF.Copy))
            ev = self.inc("scalar", "s_tre", 1)
            self.wait("sync", "s_tre", ev)
            vd = self.dma("sync", wsrow[0:1, :],
                          self.wTsb[0][0:DC, 0:128], "d_wsr")
            bv = self.dma("sync", bfrow[:], self.bfd, "d_bfr")
            self.wait("tensor", "d_wsr", vd)
            self.wait("tensor", "d_bfr", bv)
            self.wait("tensor", "v_one", ov)
            for oq in range(D // 512):
                self.mm(self.scps[:, 0:512] if D == 512 else self.scps[:, oq * 512:(oq + 1) * 512],
                        onesr[:], wsrow[:, oq * 512:(oq + 1) * 512])
            tv = self.inc("tensor", "t_bc", 1)
            self.wait("scalar", "t_bc", tv)
            self.E("scalar", lambda e, D=D: e.activation(out=wst[:], in_=self.scps[:, 0:D],
                                                         func=AF.Copy))
            sv = self.inc("scalar", "s_bc", 1)
            self.wait("tensor", "s_bc", sv)
            for oq in range(D // 512):
                self.mm(self.scps[:, oq * 512:(oq + 1) * 512],
                        onesr[:], bfrow[:, oq * 512:(oq + 1) * 512])
            tv = self.inc("tensor", "t_bc", 1)
            self.wait("scalar", "t_bc", tv)
            self.E("scalar", lambda e, D=D: e.activation(out=btt[:], in_=self.scps[:, 0:D],
                                                         func=AF.Copy))
            sv = self.inc("scalar", "s_bc", 1)
            self.wait("vector", "s_bc", sv)
            for tt_ in range(self.TTB):
                self.wait("sync", "v_out", self.val("v_out"))
                xv = self.dma("sync", x1t[:], self.x1s.ap()[tt_ * 128:(tt_ + 1) * 128, :],
                              "d_x1r")
                self.wait("vector", "d_x1r", xv)
                self.wait("vector", "d_out", self.val("d_out"))
                self.emit_ln(x1t[:])
                self.rred(self.zc1[:], self.y2[:], OP.max, absval=True)
                self.vdr()
                self.ts(self.xsc[:], self.zc1[:], 1.0 / 127.0, 1e-12, OP.mult, OP.max)
                self.vdr()
                self.recip(self.zc1[:], self.xsc[:])
                self.vdr()
                self.ts(q32[:], self.y2[:], self.zc1[:, 0:1], None, OP.mult)
                self.vdr()
                self.emit_round(q32[:], aq2[:], rr2[:], tb2[:], sgn2[:])
                xqv = self.inc("vector", "v_xq", 1)
                self.wait("tensor", "v_xq", xqv)
                for c in range(DC):
                    self.tr_evac([q32[:, c * 128:(c + 1) * 128]], xqbf[:, c, :])
                self.wait("tensor", "s_tre", self.val("s_tre"))
                self.wait("tensor", "d_wqL", self.val("d_wqL"))
                if tt_ >= 1:
                    self.wait("tensor", "v_ffo", self.val("v_ffo"))
                for c in range(DC):
                    for oq in range(D // 512):
                        self.mm(self.scps[:, oq * 512:(oq + 1) * 512],
                                xqbf[:, c, :], wqtsb[:, c, oq * 512:(oq + 1) * 512],
                                start=(c == 0), stop=(c == DC - 1))
                fv = self.inc("tensor", "t_ffn", 1)
                self.wait("vector", "t_ffn", fv)
                self.ts(self.y2[:], self.scps[:, 0:D], self.xsc[:, 0:1], None, OP.mult)
                self.vdr()
                self.tt(self.y2[:], self.y2[:], wst[:], OP.mult)
                self.vdr()
                self.tt(self.y2[:], self.y2[:], btt[:], OP.add)
                self.vdr()
                self.cp(q32[:], self.y2[:])
                yv = self.inc("vector", "v_ffo", 1)
                self.vdr()
                self.emit_ln(q32[:])
                self.tt(self.y2[:], self.y2[:], x1t[:], OP.add)
                ov2 = self.inc("vector", "v_out", 1)
                self.vdr()
                self.wait("sync", "v_out", ov2)
                self.dma("sync", self.outd[tt_ * 128:(tt_ + 1) * 128, :], self.y2[:], "d_out")
            for e in self.ENGS:
                self.wait(e, "d_out", self.val("d_out"))
            self.run_block()

    def build(self):
        self.declare_io()
        self.alloc_shared()
        self.emit_phase_a()
        self.emit_phase_b()
        self.emit_phase_c()
        self.es.close()
        return self.nc


def _host_constants(cfg):
    W = cfg["WIN"]
    BANDW = 2 * W + 128
    ident = np.eye(128, dtype=np.float32)
    r = np.arange(128)[:, None]
    c = np.arange(BANDW)[None, :]
    band = ((c >= r) & (c <= r + 2 * W)).astype(np.uint8)
    iota16 = np.broadcast_to(np.arange(16, dtype=np.float32)[None, :], (128, 16)).copy()
    return ident, band, iota16


def make_in_maps(cfg, x, Wq, Wk, Wv, Wo, Wf, bfv):
    ident, band, iota16 = _host_constants(cfg)
    HSL = cfg["HPC"] * 128
    TPC = cfg["S"] // 4
    in_maps = []
    for core in range(8):
        b = core // 4
        hc = core % 4
        in_maps.append(dict(
            xb=np.ascontiguousarray(x[b]),
            xres=np.ascontiguousarray(x[b][hc * TPC:(hc + 1) * TPC]),
            wq=np.ascontiguousarray(Wq[:, hc * HSL:(hc + 1) * HSL]),
            wk=np.ascontiguousarray(Wk[:, hc * HSL:(hc + 1) * HSL]),
            wv=np.ascontiguousarray(Wv[:, hc * HSL:(hc + 1) * HSL]),
            wo=Wo, wf=Wf, bf=bfv,
            ident=ident, bandm=band, iota16=iota16,
        ))
    return in_maps


_BUILD_CACHE = {}


def kernel(L0_attn_hidden_states, L0_attn_Wq, L0_attn_Wk, L0_attn_Wv, L0_attn_Wo,
           L0_ffn_weight, L0_ffn_bias):
    cfg = default_cfg()
    x = np.ascontiguousarray(np.asarray(L0_attn_hidden_states, dtype=np.float32))
    Wq = np.ascontiguousarray(np.asarray(L0_attn_Wq, dtype=np.float32))
    Wk = np.ascontiguousarray(np.asarray(L0_attn_Wk, dtype=np.float32))
    Wv = np.ascontiguousarray(np.asarray(L0_attn_Wv, dtype=np.float32))
    Wo = np.ascontiguousarray(np.asarray(L0_attn_Wo, dtype=np.float32))
    Wf = np.ascontiguousarray(np.asarray(L0_ffn_weight, dtype=np.float32))
    bfv = np.ascontiguousarray(np.asarray(L0_ffn_bias, dtype=np.float32)).reshape(1, -1)
    if "full" not in _BUILD_CACHE:
        _BUILD_CACHE["full"] = MK(cfg).build()
    nc = _BUILD_CACHE["full"]
    in_maps = make_in_maps(cfg, x, Wq, Wk, Wv, Wo, Wf, bfv)
    res = run_bass_kernel_spmd(nc, in_maps, core_ids=list(range(8)))
    S, D = cfg["S"], cfg["D"]
    TPC = S // 4
    out = np.empty((2, S, D), dtype=np.float32)
    for core in range(8):
        b = core // 4
        hc = core % 4
        out[b, hc * TPC:(hc + 1) * TPC] = res.results[core]["out"]
    return out


# version 08
# speedup vs baseline: 1.0105x; 1.0105x over previous
import math
import numpy as np
from contextlib import ExitStack

import concourse.bass as bass
import concourse.mybir as mybir
from concourse.bass_utils import run_bass_kernel_spmd

F32 = mybir.dt.float32
BF16 = mybir.dt.bfloat16
I8 = mybir.dt.int8
U8 = mybir.dt.uint8
AF = mybir.ActivationFunctionType
OP = mybir.AluOpType
AX = mybir.AxisListType

PHI75 = 0.674489750196082

def _A(x):
    if x is None or isinstance(x, (int, float)):
        return x
    if hasattr(x, "ap") and not isinstance(x, bass.AP):
        try:
            return x[:]
        except Exception:
            return x.ap()
    return x



def default_cfg():
    return dict(S=4096, D=2048, HPC=4, WIN=32, G=4, NPASS=5, NCORES=8, debug=0, drains=1)


class MK:
    ENGS = ("sync", "gpsimd", "tensor", "scalar", "vector")

    def __init__(self, cfg):
        self.cfg = cfg
        self.S = cfg["S"]
        self.D = cfg["D"]
        self.HPC = cfg["HPC"]
        self.WIN = cfg["WIN"]
        self.G = cfg["G"]
        self.NPASS = cfg["NPASS"]
        self.KK = max(1, self.S // 4)
        self.TT = self.S // 128
        self.DC = self.D // 128
        self.NG = self.TT // self.G
        self.TPC = self.S // 4
        self.TTB = self.TPC // 128
        self.SC = 128.0 ** -0.5
        self.BANDW = 2 * self.WIN + 128
        self.masked = self.S > 2 * self.WIN
        self.nc = bass.Bass("TRN2", target_bir_lowering=False,
                            detect_race_conditions=False)
        self.es = ExitStack()
        self.sems = {}
        self.semn = {}
        self.plan = None

    def sem(self, name):
        if name not in self.sems:
            self.sems[name] = self.es.enter_context(self.nc.semaphore(name))
            self.semn[name] = 0
        return self.sems[name]

    def val(self, name):
        self.sem(name)
        return self.semn[name]

    def E(self, eng, fn, kind="op"):
        self.plan[eng].append({"fn": fn, "kind": kind, "inc": None})

    def wait(self, eng, name, v):
        if v <= 0:
            return
        s = self.sem(name)
        self.E(eng, lambda e, s=s, v=v: e.wait_ge(s, v), kind="wait")

    def inc(self, eng, name, k=1):
        s = self.sem(name)
        self.semn[name] += k
        v = self.semn[name]
        lst = self.plan[eng]
        if lst and lst[-1]["kind"] == "op" and lst[-1]["inc"] is None:
            lst[-1]["inc"] = (s, k)
        elif eng in ("vector", "scalar"):
            self.E(eng, lambda e: e.drain(), kind="other")
            self.E(eng, lambda e, s=s, k=k: e.sem_inc(s, k), kind="sem")
        elif eng == "tensor":
            raise AssertionError(f"unattachable tensor inc for {name}")
        else:
            self.E(eng, lambda e, s=s, k=k: e.sem_inc(s, k), kind="sem")
        return v

    def sbuf(self, name, shape, dt, stack=None):
        return (stack or self.es).enter_context(self.nc.sbuf_tensor(name, shape, dt))

    def psum(self, name, shape, dt=F32):
        return self.es.enter_context(self.nc.psum_tensor(name, shape, dt))

    def barrier(self, tag):
        name = "bar_" + tag
        self.sem(name)
        tgt = self.semn[name] + 5
        self.semn[name] = tgt
        for e in self.ENGS:
            s = self.sems[name]
            lst = self.plan[e]
            if lst and lst[-1]["kind"] == "op" and lst[-1]["inc"] is None:
                lst[-1]["inc"] = (s, 1)
            elif e == "vector":
                self.E(e, lambda eng: eng.drain(), kind="other")
                self.E(e, lambda eng, s=s: eng.sem_inc(s, 1), kind="sem")
            elif e == "scalar":
                self.E(e, lambda eng: eng.drain(), kind="other")
                self.E(e, lambda eng, s=s: eng.sem_inc(s, 1), kind="sem")
            elif e == "tensor":
                self.E(e, lambda eng, s=s, b=self.trps[0], idt=self.identt:
                       eng.transpose(b[:, 0:128], idt[:], idt[:]).then_inc(s, 1),
                       kind="dma")
            else:
                self.E(e, lambda eng, s=s: eng.sem_inc(s, 1), kind="sem")
        for e in self.ENGS:
            s = self.sems[name]
            self.E(e, lambda eng, s=s, tgt=tgt: eng.wait_ge(s, tgt), kind="wait")

    @staticmethod
    def _run_entries(e, lst):
        for ent in lst:
            r = ent["fn"](e)
            if ent["inc"] is not None:
                s, k = ent["inc"]
                r.then_inc(s, k)

    def run_block(self):
        plan = self.plan
        run = self._run_entries
        with self.nc.Block() as block:
            @block.sync
            def _(e, lst=plan["sync"]):
                run(e, lst)

            @block.gpsimd
            def _(e, lst=plan["gpsimd"]):
                run(e, lst)

            @block.tensor
            def _(e, lst=plan["tensor"]):
                run(e, lst)

            @block.scalar
            def _(e, lst=plan["scalar"]):
                run(e, lst)

            @block.vector
            def _(e, lst=plan["vector"]):
                run(e, lst)

    def ts(self, dst, src, s1, s2, op0, op1=None, accum=None, eng="vector"):
        dst, src, s1, s2, accum = _A(dst), _A(src), _A(s1), _A(s2), _A(accum)
        def f(e, dst=dst, src=src, s1=s1, s2=s2, op0=op0, op1=op1, accum=accum):
            if op1 is None:
                return e.tensor_scalar(out=dst, in0=src, scalar1=s1, scalar2=s2,
                                       op0=op0, accum_out=accum)
            return e.tensor_scalar(out=dst, in0=src, scalar1=s1, scalar2=s2,
                                   op0=op0, op1=op1, accum_out=accum)
        self.E(eng, f)

    def tt(self, dst, a, b, op):
        dst, a, b = _A(dst), _A(a), _A(b)
        self.E("vector", lambda e, dst=dst, a=a, b=b, op=op:
               e.tensor_tensor(out=dst, in0=a, in1=b, op=op))

    def stt(self, dst, in0, scalar, in1, op0, op1, accum=None):
        dst, in0, scalar, in1, accum = _A(dst), _A(in0), _A(scalar), _A(in1), _A(accum)
        self.E("vector", lambda e, dst=dst, in0=in0, scalar=scalar, in1=in1,
               op0=op0, op1=op1, accum=accum:
               e.scalar_tensor_tensor(out=dst, in0=in0, scalar=scalar, in1=in1,
                                      op0=op0, op1=op1, accum_out=accum))

    def vdr(self):
        if self.cfg.get("drains"):
            self.E("vector", lambda e: e.drain(), kind="other")

    def sdr(self):
        pass

    def cp(self, dst, src):
        dst, src = _A(dst), _A(src)
        self.E("vector", lambda e, dst=dst, src=src: e.tensor_copy(dst, src))

    def cpred(self, dst, mask, data):
        dst, mask, data = _A(dst), _A(mask), _A(data)
        self.E("vector", lambda e, dst=dst, mask=mask, data=data:
               e.copy_predicated(out=dst, mask=mask, data=data))

    def act(self, dst, src, func, bias=0.0, scale=1.0, accum=None):
        dst, src, bias, accum = _A(dst), _A(src), _A(bias), _A(accum)
        self.E("scalar", lambda e, dst=dst, src=src, func=func, bias=bias,
               scale=scale, accum=accum:
               e.activation(out=dst, in_=src, func=func, bias=bias, scale=scale,
                            accum_out=accum))

    def mm(self, dst, lhsT, rhs, start=True, stop=True):
        dst, lhsT, rhs = _A(dst), _A(lhsT), _A(rhs)
        self.E("tensor", lambda e, dst=dst, lhsT=lhsT, rhs=rhs, start=start, stop=stop:
               e.matmul(dst, lhsT, rhs, start=start, stop=stop))

    def dma(self, eng, dst, src, name, cond=None, slow_ok=False):
        dst, src = _A(dst), _A(src)
        s = self.sem(name)
        self.semn[name] += 16
        if cond is None:
            self.E(eng, lambda e, d=dst, sr=src, s=s, so=slow_ok:
                   e.dma_start(out=d, in_=sr, allow_slow_non_contiguous=so)
                   .then_inc(s, 16), kind="dma")
        else:
            self.E(eng, lambda e, d=dst, sr=src, s=s, c=cond:
                   e.dma_start(out=d, in_=sr, cond=c(e)).then_inc(s, 16), kind="dma")
        return self.semn[name]

    def rred(self, dst, src, op, absval=False):
        dst, src = _A(dst), _A(src)
        self.E("vector", lambda e, dst=dst, src=src, op=op, a=absval:
               e.tensor_reduce(out=dst, in_=src, axis=AX.X, op=op, apply_absolute_value=a))

    def recip(self, dst, src):
        dst, src = _A(dst), _A(src)
        self.E("vector", lambda e, dst=dst, src=src: e.reciprocal(dst, src))

    def max8(self, dst, src):
        dst, src = _A(dst), _A(src)
        self.E("vector", lambda e, dst=dst, src=src: e.max(out=dst, in_=src))

    def mrep(self, buf, top8, immv):
        buf, top8 = _A(buf), _A(top8)
        self.E("vector", lambda e, buf=buf, top8=top8, immv=immv:
               e.match_replace(out=buf, in_to_replace=top8, in_values=buf, imm_value=immv))

    def memset(self, ap_, v):
        ap_ = _A(ap_)
        self.E("vector", lambda e, ap_=ap_, v=v: e.memset(ap_, v))

    def tr_evac(self, srcs, dst, scale=1.0):
        srcs = [_A(s) for s in srcs]
        dst = _A(dst)
        """srcs: list of up to 4 [128 x 128] SBUF APs -> transposed into one
        trps bank -> single scalar evac (with optional scale/dtype cast) to dst
        ([128 x 128*len]). Returns after emitting; sync via t_tra/s_tre."""
        n = len(srcs)
        bank = self.trps[self._trp_par % 2]
        self._trp_par += 1
        self.wait("tensor", "s_tre", self.val("s_tre") - 1)
        for j, sap in enumerate(srcs):
            self.E("tensor", lambda e, b=bank, j=j, sap=sap, idt=self.identt:
                   e.transpose(b[:, j * 128:(j + 1) * 128], sap, idt[:]))
        tv = self.inc("tensor", "t_tra", 1)
        self.wait("scalar", "t_tra", tv)
        self.E("scalar", lambda e, b=bank, dst=dst, n=n, sc=scale:
               e.activation(out=dst, in_=b[:, 0:n * 128], func=AF.Copy, scale=sc))
        ev = self.inc("scalar", "s_tre", 1)
        self._last_ttra = tv
        return ev

    def emit_ln(self, src_ap):
        D = self.D
        St = self.St
        self.rred(self.zc1[:], src_ap, OP.add)
        v1 = self.inc("vector", "v_ln1", 1)
        self.wait("scalar", "v_ln1", v1)
        self.act(self.trash_s[:, 0:D], src_ap, AF.Square, accum=St["tm1"][:, 0:1])
        s1 = self.inc("scalar", "s_ln1", 1)
        self.wait("vector", "s_ln1", s1)
        self.ts(self.zc1[:], self.zc1[:], 1.0 / D, None, OP.mult)
        self.ts(St["tm2"][:, 0:1], St["tm1"][:, 0:1], 1.0 / D, None, OP.mult)
        self.vdr()
        self.tt(St["tm3"][:, 0:1], self.zc1[:], self.zc1[:], OP.mult)
        self.vdr()
        self.tt(St["tm2"][:, 0:1], St["tm2"][:, 0:1], St["tm3"][:, 0:1], OP.subtract)
        v2 = self.inc("vector", "v_ln2", 1)
        self.wait("scalar", "v_ln2", v2)
        self.act(St["tm3"][:, 0:1], St["tm2"][:, 0:1], AF.Sqrt, bias=self.epst[:, 0:1])
        s2 = self.inc("scalar", "s_ln2", 1)
        self.wait("vector", "s_ln2", s2)
        self.recip(St["tm4"][:, 0:1], St["tm3"][:, 0:1])
        self.vdr()
        self.tt(St["tm3"][:, 0:1], self.zc1[:], St["tm4"][:, 0:1], OP.mult)
        self.vdr()
        self.ts(St["tm3"][:, 0:1], St["tm3"][:, 0:1], -1.0, None, OP.mult)
        self.vdr()
        self.ts(self.y2[:], src_ap, St["tm4"][:, 0:1], St["tm3"][:, 0:1], OP.mult, OP.add)
        self.vdr()

    def emit_round(self, q, a, rr, tb, sgn):
        self.stt(a, q, -1.0, q, OP.mult, OP.max)
        qv = self.inc("vector", "v_rnd", 1)
        self.wait("scalar", "v_rnd", qv)
        self.act(sgn, q, AF.Sign)
        sv = self.inc("scalar", "s_rnd", 1)
        self.ts(rr, a, 0.5, None, OP.add)
        self.ts(a, a, 0.5, None, OP.add)
        self.vdr()
        for b in (64.0, 32.0, 16.0, 8.0, 4.0, 2.0, 1.0):
            self.ts(tb, rr, b, -b, OP.is_ge, OP.mult)
            self.tt(rr, rr, tb, OP.add)
            self.vdr()
        self.tt(a, a, rr, OP.subtract)
        self.vdr()
        self.wait("vector", "s_rnd", sv)
        self.tt(q, a, sgn, OP.mult)

    def declare_io(self):
        nc, S, D = self.nc, self.S, self.D
        HSL = self.HPC * 128
        self.xb = nc.dram_tensor("xb", [S, D], F32, kind="ExternalInput").ap()
        self.xres = nc.dram_tensor("xres", [self.TPC, D], F32, kind="ExternalInput").ap()
        self.wq = nc.dram_tensor("wq", [D, HSL], F32, kind="ExternalInput").ap()
        self.wk = nc.dram_tensor("wk", [D, HSL], F32, kind="ExternalInput").ap()
        self.wv = nc.dram_tensor("wv", [D, HSL], F32, kind="ExternalInput").ap()
        self.wod = nc.dram_tensor("wo", [D, D], F32, kind="ExternalInput").ap()
        self.wfd = nc.dram_tensor("wf", [D, D], F32, kind="ExternalInput").ap()
        self.bfd = nc.dram_tensor("bf", [1, D], F32, kind="ExternalInput").ap()
        self.identd = nc.dram_tensor("ident", [128, 128], F32, kind="ExternalInput").ap()
        self.bandd = nc.dram_tensor("bandm", [128, self.BANDW], U8, kind="ExternalInput").ap()
        self.iotad = nc.dram_tensor("iota16", [128, 16], F32, kind="ExternalInput").ap()
        self.outd = nc.dram_tensor("out", [self.TPC, D], F32, kind="ExternalOutput").ap()
        self.ag_in = nc.dram_tensor("ag_in", [HSL, S], F32)
        self.ag_out = nc.dram_tensor("ag_out", [4 * HSL, S], F32)
        self.x1s = nc.dram_tensor("x1s", [self.TPC, D], F32)
        self.wqts = nc.dram_tensor("wqts", [self.DC, self.DC, 128, 128], BF16)
        if self.cfg.get("debug"):
            self.dbg_w = nc.dram_tensor("dbg_w", [self.HPC * self.TT, 128, S], F32,
                                        kind="ExternalOutput").ap()
            self.dbg_fin = nc.dram_tensor("dbg_fin", [self.HPC * self.TT, 128, 16], F32,
                                          kind="ExternalOutput").ap()
            self.dbg_stt = nc.dram_tensor("dbg_stt", [self.HPC * self.TT, 128, 8], F32,
                                          kind="ExternalOutput").ap()

    def alloc_shared(self):
        sb = self.sbuf
        G = self.G
        self.identt = sb("identt", [128, 128], F32)
        self.bandt = sb("bandt", [128, self.BANDW], U8)
        self.iotat = sb("iotat", [128, 16], F32)
        self.epst = sb("epst", [128, 1], F32)
        self.trash_s = sb("trash_s", [128, self.S], I8)
        self.zc1 = sb("zc1", [128, 1], F32)
        self.zcand = sb("zcand", [128, 1], F32)
        self.xsc = sb("xsc", [128, 1], F32)
        self.wTsb = [sb(f"wTsb{i}", [128, 512], F32) for i in range(2)]
        self.oTst = [sb(f"oTst{i}", [128, 128], F32) for i in range(2)]
        self.ws16 = sb("ws16", [128, self.DC], F32)
        self.y2 = sb("y2", [128, self.D], F32)
        names = ("t", "c", "lot", "loc", "hit", "hic", "hvl", "hvh", "mu", "sg",
                 "rsg", "CC", "negC", "th", "nth", "e2", "thhi",
                 "tm1", "tm2", "tm3", "tm4", "dens", "rdens", "tup", "tdn")
        self.StP = [{n: sb(f"st{p}_" + n, [128, G], F32) for n in names}
                    for p in range(3)]
        self.Mi8P = [{n: sb(f"m8{p}_" + n, [128, G], I8) for n in ("a", "b", "c", "d")}
                     for p in range(3)]
        self.St = self.StP[0]
        self.Mi8 = self.Mi8P[0]
        self.scps = self.psum("scps", [128, max(2 * min(1024, self.S), self.D)], F32)
        self.trps = [self.psum(f"trps{i}", [128, 512], F32) for i in range(2)]
        self.pvps = self.psum("pvps", [128, 512], F32)
        self.bps = self.psum("bps", [128, 512], F32)
        self._trp_par = 0
        self.zrel = {}
        self._qk_q = 0
        self._last_ttra = 0
        self._xn_rel = 0
        self._odv_rel = {0: 0, 1: 0}
        self._pv_done = {}
        self._zrec_done = {}
        self._exp_done = {}

    def alloc_a(self, st):
        sb = self.sbuf
        S, D, DC, TT, G = self.S, self.D, self.DC, self.TT, self.G
        self.xt = [sb(f"xt{i}", [128, D], F32, st) for i in range(2)]
        self.xn = sb("xn", [128, D], F32, st)
        self.lnT = [sb("lnT0", [128, DC, 128], F32, st)] * 2
        self.rstdc = sb("rstdc", [128, TT], F32, st)
        self.negmrc = sb("negmrc", [128, TT], F32, st)
        self.wqh = sb("wqh", [128, DC, 128], F32, st)
        self.wkh = sb("wkh", [128, DC, 128], F32, st)
        self.wvh = sb("wvh", [128, DC, 128], F32, st)
        self.qT = sb("qT", [128, S], F32, st)
        self.kT = sb("kT", [128, S], F32, st)
        self.vT = sb("vT", [128, TT, 128], F32, st)
        self.zt = sb("zt", [128, G, S], F32, st)
        self.trash_v = sb("trash_v", [128, S], I8, st)
        self.ubuf = sb("ubuf", [128, S // 4], F32, st)
        self.cand = sb("cand", [128, 32], F32, st)
        self.fin16 = sb("fin16", [128, 16], F32, st)
        self.zb = sb("zb", [128, self.BANDW], F32, st)
        self.gsb = sb("gsb", [128, 128], F32, st)
        self.kbar = sb("kbar", [128, 1], F32, st)
        self.odv = [sb(f"odv{i}", [128, 128], F32, st) for i in range(2)]
        self.wqbf = [sb(f"wqbf{i}", [128, 128], BF16, st) for i in range(2)]
        self.qtm = sb("qtm", [128, 128], F32, st)
        self.zrec = sb("zrec", [128, G], F32, st)

    def emit_consts(self):
        v = self.dma("sync", self.identt[:], self.identd, "d_const")
        v = self.dma("sync", self.bandt[:], self.bandd, "d_const")
        v = self.dma("sync", self.iotat[:], self.iotad, "d_const")
        for e in ("vector", "scalar", "tensor"):
            self.wait(e, "d_const", v)
        self.memset(self.epst[:], 1e-5)
        vc = self.inc("vector", "v_const", 1)
        self.wait("scalar", "v_const", vc)
        self.wait("tensor", "v_const", vc)
        self.E("scalar", lambda e: e.preload_act_table(AF.Sqrt), kind="other")

    def emit_build_head(self, h):
        DC, TT, D = self.DC, self.TT, self.D
        self.wait("sync", "t_pj", 3 * self.TT * h)
        pvv = self.val("t_pvg")
        self.wait("vector", "t_pvg", pvv)
        self.wait("scalar", "t_pvg", pvv)
        for w_dram, w_sb in ((self.wq, self.wqh), (self.wk, self.wkh), (self.wv, self.wvh)):
            src = w_dram[:, h * 128:(h + 1) * 128].rearrange("(c p) n -> p c n", p=128)
            self.dma("sync", w_sb[:], src, "d_wh")
        self.wait("tensor", "d_wh", self.val("d_wh"))
        for i in range(TT):
            buf = self.xt[i % 2]
            gidx = h * TT + i
            self.wait("sync", "v_nrm", gidx - 1)
            dxn = f"d_x{i % 2}"
            vx = self.dma("sync", buf[:], self.xb[i * 128:(i + 1) * 128, :], dxn)
            self.wait("vector", dxn, vx)
            St = self.St
            if h == 0:
                self.rred(self.zc1[:], buf[:], OP.add)
                v1 = self.inc("vector", "v_bs1", 1)
                self.wait("scalar", "v_bs1", v1)
                self.wait("scalar", dxn, vx)
                self.act(self.trash_s[:, 0:D], buf[:], AF.Square, accum=St["tm1"][:, 0:1])
                s1 = self.inc("scalar", "s_bs1", 1)
                self.wait("vector", "s_bs1", s1)
                self.ts(self.zc1[:], self.zc1[:], 1.0 / D, None, OP.mult)
                self.ts(St["tm2"][:, 0:1], St["tm1"][:, 0:1], 1.0 / D, None, OP.mult)
                self.vdr()
                self.tt(St["tm3"][:, 0:1], self.zc1[:], self.zc1[:], OP.mult)
                self.vdr()
                self.tt(St["tm2"][:, 0:1], St["tm2"][:, 0:1], St["tm3"][:, 0:1], OP.subtract)
                v2 = self.inc("vector", "v_bs2", 1)
                self.wait("scalar", "v_bs2", v2)
                self.act(St["tm3"][:, 0:1], St["tm2"][:, 0:1], AF.Sqrt, bias=self.epst[:, 0:1])
                s2 = self.inc("scalar", "s_bs2", 1)
                self.wait("vector", "s_bs2", s2)
                self.recip(self.rstdc[:, i:i + 1], St["tm3"][:, 0:1])
                self.vdr()
                self.tt(St["tm3"][:, 0:1], self.zc1[:], self.rstdc[:, i:i + 1], OP.mult)
                self.vdr()
                self.ts(self.negmrc[:, i:i + 1], St["tm3"][:, 0:1], -1.0, None, OP.mult)
                self.vdr()
            self.wait("vector", "t_tra", self._xn_rel)
            self.ts(self.xn[:], buf[:], self.rstdc[:, i:i + 1], self.negmrc[:, i:i + 1],
                    OP.mult, OP.add)
            vn = self.inc("vector", "v_nrm", 1)
            self.wait("tensor", "v_nrm", vn)
            lnb = self.lnT[i % 2]
            self.wait("scalar", "t_pj", 3 * gidx)
            for g4 in range(DC // 4):
                self.tr_evac([self.xn[:, (g4 * 4 + j) * 128:(g4 * 4 + j + 1) * 128]
                              for j in range(4)],
                             lnb[:, g4 * 4:(g4 + 1) * 4, :])
            self._xn_rel = self.val("t_tra")
            self.wait("tensor", "s_tre", self.val("s_tre"))
            self.wait("tensor", "s_pe", 3 * gidx)
            for c in range(DC):
                self.mm(self.scps[:, 0:128], self.wqh[:, c, :], lnb[:, c, :],
                        start=(c == 0), stop=(c == DC - 1))
            self.inc("tensor", "t_pj", 1)
            for c in range(DC):
                self.mm(self.scps[:, 512:640], self.wkh[:, c, :], lnb[:, c, :],
                        start=(c == 0), stop=(c == DC - 1))
            self.inc("tensor", "t_pj", 1)
            for c in range(DC):
                self.mm(self.scps[:, 1024:1152], lnb[:, c, :], self.wvh[:, c, :],
                        start=(c == 0), stop=(c == DC - 1))
            tpj = self.inc("tensor", "t_pj", 1)
            self.wait("scalar", "t_pj", tpj - 2)
            self.E("scalar", lambda e, i=i: e.activation(
                out=self.qT[:, i * 128:(i + 1) * 128], in_=self.scps[:, 0:128], func=AF.Copy))
            self.inc("scalar", "s_pe", 1)
            self.wait("scalar", "t_pj", tpj - 1)
            self.E("scalar", lambda e, i=i: e.activation(
                out=self.kT[:, i * 128:(i + 1) * 128], in_=self.scps[:, 512:640], func=AF.Copy))
            self.inc("scalar", "s_pe", 1)
            self.wait("scalar", "t_pj", tpj)
            self.E("scalar", lambda e, i=i: e.activation(
                out=self.vT[:, i, 0:128], in_=self.scps[:, 1024:1152], func=AF.Copy))
            self.inc("scalar", "s_pe", 1)

    def emit_head_stats(self, h):
        TT = self.TT
        self.wait("vector", "s_pe", self.val("s_pe"))
        self.wait("vector", "t_st", self.val("t_st"))
        self.rred(self.kbar[:], self.kT[:], OP.add)
        kv = self.inc("vector", "v_kb", 1)
        self.wait("tensor", "v_kb", kv)
        for i in range(TT):
            dst = self.wTsb[i % 2][:, 0:128]
            self.wait("scalar", "t_gk", self.val("t_gk") - 1)
            ev = self.tr_evac([self.kT[:, i * 128:(i + 1) * 128]], dst)
            self.wait("tensor", "s_tre", ev)
            self.mm(self.bps[:, 0:128], dst, dst, start=(i == 0), stop=(i == TT - 1))
            self.inc("tensor", "t_gk", 1)
        self.wait("scalar", "t_gk", self.val("t_gk"))
        self.E("scalar", lambda e, sc=self.SC * self.SC / self.S:
               e.activation(out=self.gsb[:], in_=self.bps[:, 0:128], func=AF.Copy, scale=sc))
        self.inc("scalar", "s_g", 1)

    def emit_group_stats(self, h, g):
        St, G, S = self.St, self.G, self.S
        self.wait("vector", "s_exp", self.val("s_exp"))
        self.wait("vector", "s_cnt", self.val("s_cnt"))
        self.wait("vector", "s_thh", self.val("s_thh"))
        self.wait("tensor", "s_g", self.val("s_g"))
        for gi in range(G):
            i = g * G + gi
            qc = self.qT[:, i * 128:(i + 1) * 128]
            self.wait("tensor", "s_st", self.val("s_st"))
            self.wait("tensor", "v_st", self.val("v_st"))
            self.mm(self.bps[:, 384 + gi:385 + gi], qc, self.kbar[:])
            self.mm(self.bps[:, 0:128], qc, self.gsb[:])
            self.wait("scalar", "v_st", self.val("v_st"))
            ev = self.tr_evac([qc], self.qtm[:])
            tv = self._last_ttra
            self.wait("scalar", "t_tra", tv)
            self.E("scalar", lambda e, gi=gi, sc=self.SC / self.S:
                   e.activation(out=St["mu"][:, gi:gi + 1],
                                in_=self.bps[:, 384 + gi:385 + gi], func=AF.Copy, scale=sc))
            sv = self.inc("scalar", "s_st", 1)
            self.wait("vector", "s_st", sv)
            self.wait("vector", "s_tre", ev)
            self.wait("vector", "t_tra", tv)
            self.stt(self.qtm[:], self.bps[:, 0:128], 1.0, self.qtm[:], OP.mult, OP.mult,
                     accum=St["e2"][:, gi:gi + 1])
            self.inc("vector", "v_st", 1)
            self.vdr()
        self.tt(St["tm1"], St["mu"], St["mu"], OP.mult)
        self.vdr()
        self.tt(St["tm2"], St["e2"], St["tm1"], OP.subtract)
        self.vdr()
        self.ts(St["tm2"], St["tm2"], 1e-12, None, OP.max)
        self.vdr()
        self.ts(St["tm3"], St["tm2"], 1.0, None, OP.add)
        self.vdr()
        self.recip(St["tm4"], St["tm3"])
        self.vdr()
        self.ts(St["tm4"], St["tm4"], 2.0, None, OP.mult)
        self.vdr()
        for _ in range(4):
            self.tt(St["tm3"], St["tm4"], St["tm4"], OP.mult)
            self.vdr()
            self.tt(St["tm3"], St["tm3"], St["tm2"], OP.mult)
            self.vdr()
            self.ts(St["tm3"], St["tm3"], -0.5, 1.5, OP.mult, OP.add)
            self.vdr()
            self.tt(St["tm4"], St["tm4"], St["tm3"], OP.mult)
            self.vdr()
        self.cp(St["rsg"], St["tm4"])
        self.tt(St["sg"], St["tm2"], St["tm4"], OP.mult)
        self.vdr()
        self.ts(St["tm1"], St["sg"], 6.0, None, OP.mult)
        self.vdr()
        self.tt(St["CC"], St["mu"], St["tm1"], OP.add)
        self.vdr()
        self.ts(St["negC"], St["CC"], -1.0, None, OP.mult)
        self.ts(St["tm1"], St["sg"], PHI75, None, OP.mult)
        self.vdr()
        self.tt(St["t"], St["mu"], St["tm1"], OP.add)
        self.vdr()
        self.tt(St["tm1"], St["t"], St["CC"], OP.subtract)
        vv = self.inc("vector", "v_th", 1)
        self.wait("scalar", "v_th", vv)
        self.act(St["th"], St["tm1"], AF.Exp)
        sv = self.inc("scalar", "s_thx", 1)
        self.wait("vector", "s_thx", sv)
        self.ts(St["nth"], St["th"], -1.0, None, OP.mult)
        for nm, v in (("lot", 0.0), ("loc", float(S)), ("hit", 0.0), ("hic", 0.0),
                      ("hvl", 0.0), ("hvh", 0.0)):
            self.memset(St[nm][:], v)
        self.inc("vector", "v_gini", 1)
        self.vdr()

    def emit_qk_slot(self, h, g, gi):
        G, S = self.G, self.S
        nC = self.St["negC"]
        QW = min(1024, S)
        nq = S // QW
        i = g * G + gi
        qc = self.qT[:, i * 128:(i + 1) * 128]
        prev_tile = h * self.TT + i - G
        if prev_tile in self.zrel:
            self.wait("scalar", "t_tra", self.zrel[prev_tile])
        self.wait("scalar", "v_gini", self.val("v_gini"))
        for qq in range(nq):
            reg = QW * (self._qk_q % 2)
            self._qk_q += 1
            self.wait("tensor", "s_exp", self.val("s_exp") - 1)
            for cc in range(QW // 512):
                k0 = qq * QW + cc * 512
                self.mm(self.scps[:, reg + cc * 512: reg + (cc + 1) * 512],
                        qc, self.kT[:, k0:k0 + 512])
            tv = self.inc("tensor", "t_qk", 1)
            self.wait("scalar", "t_qk", tv)
            self.E("scalar", lambda e, gi=gi, qq=qq, reg=reg, QW=QW, sc=self.SC, nC=nC:
                   e.activation(out=self.zt[:, gi, qq * QW:(qq + 1) * QW],
                                in_=self.scps[:, reg:reg + QW], func=AF.Exp,
                                bias=nC[:, gi:gi + 1], scale=sc))
            self.inc("scalar", "s_exp", 1)
        self._exp_done[(h, g, gi)] = self.val("s_exp")

    def emit_group_counts(self, h, g):
        St, G, S = self.St, self.G, self.S
        half = G // 2
        for it in range(self.NPASS):
            self.wait("scalar", "v_nthx", self.val("v_nthx"))
            self.wait("scalar", "s_exp", self.val("s_exp"))
            for gi in range(half, G):
                self.act(self.trash_s[:], self.zt[:, gi, :], AF.Sign,
                         bias=St["nth"][:, gi:gi + 1], accum=St["c"][:, gi:gi + 1])
            cv = self.inc("scalar", "s_cnt", 1)
            ed = self._exp_done.get((h, g, half - 1), self.val("s_exp"))
            self.wait("vector", "s_exp", ed)
            self.wait("vector", "s_thx", self.val("s_thx"))
            for gi in range(half):
                self.ts(self.trash_v[:], self.zt[:, gi, :], St["th"][:, gi:gi + 1],
                        None, OP.is_gt, OP.add, accum=St["c"][:, gi:gi + 1])
            self.vdr()
            self.wait("vector", "s_cnt", cv)
            self.ts(St["c"][:, half:G], St["c"][:, half:G], 0.5, S / 2.0, OP.mult, OP.add)
            self.vdr()
            self.emit_update(last=(it == self.NPASS - 1))

    def emit_update(self, last):
        St, M = self.St, self.Mi8
        KK, S = self.KK, self.S
        T_AIM = KK - 4.5
        self.ts(St["tm1"], St["c"], KK - 0.5, None, OP.is_le)
        self.vdr()
        self.cp(M["a"][:], St["tm1"])
        self.vdr()
        self.cpred(St["hit"], M["a"][:], St["t"])
        self.cpred(St["hic"], M["a"][:], St["c"])
        self.tt(St["hvh"], St["hvh"], St["tm1"], OP.max)
        self.vdr()
        self.ts(St["tm2"], St["tm1"], -1.0, 1.0, OP.mult, OP.add)
        self.vdr()
        self.cp(M["b"][:], St["tm2"])
        self.vdr()
        self.cpred(St["lot"], M["b"][:], St["t"])
        self.cpred(St["loc"], M["b"][:], St["c"])
        self.tt(St["hvl"], St["hvl"], St["tm2"], OP.max)
        self.vdr()
        if last:
            self.tt(St["tm1"], St["hit"], St["CC"], OP.subtract)
            vv = self.inc("vector", "v_thh", 1)
            self.wait("scalar", "v_thh", vv)
            self.act(St["thhi"], St["tm1"], AF.Exp)
            self.inc("scalar", "s_thh", 1)
            return
        self.tt(St["tm1"], St["t"], St["mu"], OP.subtract)
        self.vdr()
        self.tt(St["tm1"], St["tm1"], St["rsg"], OP.mult)
        vv = self.inc("vector", "v_dz", 1)
        self.wait("scalar", "v_dz", vv)
        self.act(St["tm2"], St["tm1"], AF.Square)
        self.sdr()
        self.act(St["tm2"], St["tm2"], AF.Exp, scale=-0.5)
        sv = self.inc("scalar", "s_dz", 1)
        self.wait("vector", "s_dz", sv)
        self.ts(St["tm2"], St["tm2"], S / math.sqrt(2 * math.pi), None, OP.mult)
        self.vdr()
        self.tt(St["dens"], St["tm2"], St["rsg"], OP.mult)
        self.vdr()
        self.ts(St["dens"], St["dens"], 8.0, None, OP.max)
        self.vdr()
        self.recip(St["rdens"], St["dens"])
        self.vdr()
        self.ts(St["tm1"], St["loc"], KK - 12.0, None, OP.subtract)
        self.vdr()
        self.tt(St["tm1"], St["tm1"], St["rdens"], OP.mult)
        self.vdr()
        self.tt(St["tup"], St["lot"], St["tm1"], OP.add)
        self.ts(St["tm2"], St["hic"], -1.0, T_AIM, OP.mult, OP.add)
        self.vdr()
        self.tt(St["tm2"], St["tm2"], St["rdens"], OP.mult)
        self.vdr()
        self.tt(St["tdn"], St["hit"], St["tm2"], OP.subtract)
        self.vdr()
        self.tt(St["tm1"], St["loc"], St["hic"], OP.subtract)
        self.vdr()
        self.recip(St["tm2"], St["tm1"])
        self.vdr()
        self.ts(St["tm1"], St["loc"], T_AIM, None, OP.subtract)
        self.vdr()
        self.tt(St["tm1"], St["tm1"], St["tm2"], OP.mult)
        self.vdr()
        self.ts(St["tm1"], St["tm1"], 0.05, 0.95, OP.max, OP.min)
        self.vdr()
        self.tt(St["tm3"], St["hit"], St["lot"], OP.subtract)
        self.vdr()
        self.tt(St["tm1"], St["tm1"], St["tm3"], OP.mult)
        self.vdr()
        self.tt(St["tm1"], St["lot"], St["tm1"], OP.add)
        self.vdr()
        self.cp(M["b"][:], St["hvl"])
        self.tt(St["tm2"], St["hvl"], St["hvh"], OP.mult)
        self.vdr()
        self.cp(M["c"][:], St["tm2"])
        self.ts(St["tm3"], St["hic"], KK - 8.5, None, OP.is_ge)
        self.vdr()
        self.tt(St["tm3"], St["tm3"], St["hvh"], OP.mult)
        self.vdr()
        self.cp(M["d"][:], St["tm3"])
        self.vdr()
        self.cp(St["t"], St["tdn"])
        self.vdr()
        self.cpred(St["t"], M["b"][:], St["tup"])
        self.vdr()
        self.cpred(St["t"], M["c"][:], St["tm1"])
        self.vdr()
        self.cpred(St["t"], M["d"][:], St["hit"])
        self.vdr()
        self.tt(St["tm1"], St["t"], St["CC"], OP.subtract)
        vv = self.inc("vector", "v_th2", 1)
        self.wait("scalar", "v_th2", vv)
        self.act(St["th"], St["tm1"], AF.Exp)
        sv = self.inc("scalar", "s_thx", 1)
        self.wait("vector", "s_thx", sv)
        self.ts(St["nth"], St["th"], -1.0, None, OP.mult)
        self.inc("vector", "v_nthx", 1)
        self.vdr()

    def band_slices(self, i):
        W, S = self.WIN, self.S
        k0 = i * 128 - W
        k1 = i * 128 + 128 + W
        b0, b1 = max(0, k0), min(S, k1)
        m0 = b0 - k0
        return b0, b1, m0, m0 + (b1 - b0)

    def emit_tile_finish(self, h, g, gi):
        St, S, KK = self.St, self.S, self.KK
        i = g * self.G + gi
        zrow = self.zt[:, gi, :]
        QW = S // 4
        self.wait("vector", "s_thh", self.val("s_thh"))
        self.ts(self.trash_v[:], zrow, St["thhi"][:, gi:gi + 1], None,
                OP.is_gt, OP.add, accum=St["c"][:, gi:gi + 1])
        self.vdr()
        self.ts(self.zc1[:], St["c"][:, gi:gi + 1], -1.0, float(KK - 1), OP.mult, OP.add)
        self.vdr()
        self.ts(self.zc1[:], self.zc1[:], 0.0, 15.0, OP.max, OP.min)
        self.vdr()
        if self.masked:
            for qq in range(4):
                zq = self.zt[:, gi, qq * QW:(qq + 1) * QW]
                self.stt(self.ubuf[:], zq, St["thhi"][:, gi:gi + 1], zq, OP.is_le, OP.mult)
                self.max8(self.cand[:, qq * 8:(qq + 1) * 8], self.ubuf[:])
            self.vdr()
            self.max8(self.fin16[:, 0:8], self.cand[:])
            self.vdr()
            self.mrep(self.cand[:], self.fin16[:, 0:8], -1.0)
            self.vdr()
            self.max8(self.fin16[:, 8:16], self.cand[:])
            self.vdr()
            self.ts(self.cand[:, 0:16], self.iotat[:], self.zc1[:, 0:1], None, OP.subtract)
            self.vdr()
            self.tt(self.cand[:, 0:16], self.cand[:, 0:16], self.cand[:, 0:16], OP.mult)
            self.vdr()
            self.ts(self.cand[:, 0:16], self.cand[:, 0:16], 0.0625, None, OP.is_le)
            self.vdr()
            self.tt(self.cand[:, 0:16], self.cand[:, 0:16], self.fin16[:], OP.mult)
            self.vdr()
            self.rred(self.zcand[:], self.cand[:, 0:16], OP.add)
            self.vdr()
            b0, b1, m0, m1 = self.band_slices(i)
            bw = b1 - b0
            self.cp(self.zb[:, 0:bw], self.zt[:, gi, b0:b1])
            self.vdr()
            self.stt(zrow, zrow, self.zcand[:, 0:1], zrow, OP.is_ge, OP.mult,
                     accum=St["tm1"][:, gi:gi + 1])
            self.vdr()
            self.rred(St["tm2"][:, gi:gi + 1], self.zt[:, gi, b0:b1], OP.add)
            self.vdr()
            self.cpred(self.zt[:, gi, b0:b1], self.bandt[:, m0:m1], self.zb[:, 0:bw])
            self.vdr()
            self.rred(St["tm3"][:, gi:gi + 1], self.zt[:, gi, b0:b1], OP.add)
            self.vdr()
            self.tt(St["tm1"][:, gi:gi + 1], St["tm1"][:, gi:gi + 1],
                    St["tm2"][:, gi:gi + 1], OP.subtract)
            self.vdr()
            self.tt(St["tm1"][:, gi:gi + 1], St["tm1"][:, gi:gi + 1],
                    St["tm3"][:, gi:gi + 1], OP.add)
            self.vdr()
            self.recip(self.zrec[:, gi:gi + 1], St["tm1"][:, gi:gi + 1])
        else:
            self.rred(St["tm1"][:, gi:gi + 1], zrow, OP.add)
            self.vdr()
            self.recip(self.zrec[:, gi:gi + 1], St["tm1"][:, gi:gi + 1])
        self._zrec_done[h * self.TT + i] = self.inc("vector", "v_zr", 1)
        if self.cfg.get("debug"):
            tg = h * self.TT + i
            fv = self.inc("vector", "v_fin", 1)
            self.wait("sync", "v_fin", fv)
            self.dma("sync", self.dbg_w[tg], self.zt[:, gi, :], "d_dbg")
            self.dma("sync", self.dbg_fin[tg], self.fin16[:], "d_dbg")
            for jj, ap_ in ((0, St["c"][:, gi:gi + 1]), (1, St["thhi"][:, gi:gi + 1]),
                            (2, self.zcand[:]), (3, self.zc1[:]),
                            (4, St["hit"][:, gi:gi + 1]), (5, St["hic"][:, gi:gi + 1])):
                self.dma("sync", self.dbg_stt[tg, :, jj:jj + 1], ap_, "d_dbg", slow_ok=True)
            return
        self.inc("vector", "v_fin", 1)

    def emit_tile_pv_mms(self, h, g, gi):
        TT = self.TT
        i = g * self.G + gi
        tile_glob = h * TT + i
        self.wait("tensor", "v_fin", self.val("v_fin"))
        if self.cfg.get("debug"):
            self.wait("vector", "d_dbg", self.val("d_dbg"))
        pvoff = 128 * (tile_glob % 4)
        self.wait("tensor", "s_pve", self.val("s_pve"))
        ngrp = (TT + 3) // 4
        if self.D >= 2048:
            stg = [self.y2[:, j * 512:(j + 1) * 512] for j in range(4)]
        else:
            stg = [self.wTsb[0][:], self.wTsb[1][:]]
        NS = len(stg)
        V0 = self.val("t_pvg")
        evs = {}

        def emit_T(j):
            ks = [j * 4 + kk for kk in range(4) if j * 4 + kk < TT]
            self.wait("scalar", "t_pvg", V0 + j - NS + 1)
            evs[j] = self.tr_evac([self.zt[:, gi, k * 128:(k + 1) * 128] for k in ks],
                                  stg[j % NS][:, 0:len(ks) * 128])

        emit_T(0)
        for grp in range(ngrp):
            if grp + 1 < ngrp:
                emit_T(grp + 1)
            ks = [grp * 4 + kk for kk in range(4) if grp * 4 + kk < TT]
            self.wait("tensor", "s_tre", evs[grp])
            for jj, k in enumerate(ks):
                self.mm(self.pvps[:, pvoff:pvoff + 128],
                        stg[grp % NS][:, jj * 128:(jj + 1) * 128],
                        self.vT[:, k, 0:128],
                        start=(grp == 0 and jj == 0),
                        stop=(grp == ngrp - 1 and jj == len(ks) - 1))
            self.inc("tensor", "t_pvg", 1)
        self.zrel[tile_glob] = self.val("t_tra")
        self._pv_done[tile_glob] = self.val("t_pvg")

    def emit_tile_out(self, h, g, gi):
        TT = self.TT
        i = g * self.G + gi
        tile_glob = h * TT + i
        pvoff = 128 * (tile_glob % 4)
        par = tile_glob % 2
        self.wait("scalar", "t_pvg", self._pv_done[tile_glob])
        self.wait("scalar", "v_zr", self._zrec_done[tile_glob])
        agn = f"d_ag{par}"
        self.wait("scalar", agn, self.val(agn))
        self.wait("scalar", "t_tra", self._odv_rel[par])
        self.E("scalar", lambda e, pvoff=pvoff, gi=gi, dst=self.odv[par]:
               e.activation(out=dst[:], in_=self.pvps[:, pvoff:pvoff + 128],
                            func=AF.Copy, scale=self.zrec[:, gi:gi + 1]))
        sv = self.inc("scalar", "s_pve", 1)
        self.wait("tensor", "s_pve", sv)
        ev = self.tr_evac([self.odv[par][:]], self.oTst[par][:])
        self._odv_rel[par] = self.val("t_tra")
        self.wait("scalar", "s_tre", ev)
        s = self.sem(agn)
        self.semn[agn] += 16
        self.E("scalar", lambda e, src=self.oTst[par], h=h, i=i, s=s:
               e.dma_start(out=self.ag_in.ap()[h * 128:(h + 1) * 128, i * 128:(i + 1) * 128],
                           in_=src[:]).then_inc(s, 16))

    def emit_wquant(self, h):
        DC, D = self.DC, self.D
        per = (DC + self.HPC - 1) // self.HPC
        tiles = range(h * per, min(DC, (h + 1) * per))
        wft = self.zt[:, 0, 0:D]
        aq = self.zt[:, 1, 0:D]
        rres = self.zt[:, 2, 0:D]
        if 2 * D <= self.S:
            sg_t = self.zt[:, 1, D:2 * D]
            tb = self.zt[:, 0, D:2 * D]
        else:
            raise AssertionError("wquant scratch layout needs 2*D <= S")
        rel = self.val("t_tra")
        self.wait("sync", "t_tra", rel)
        self.wait("vector", "t_tra", rel)
        for ti in tiles:
            self.wait("sync", "v_wq", self.val("v_wq"))
            self.wait("sync", "t_tra", self.val("t_tra"))
            vd = self.dma("sync", wft, self.wfd[ti * 128:(ti + 1) * 128, :], "d_wf")
            self.wait("vector", "d_wf", vd)
            self.rred(self.zc1[:], wft, OP.max, absval=True)
            self.vdr()
            self.ts(self.zc1[:], self.zc1[:], 1.0 / 127.0, 1e-8 / 127.0, OP.mult, OP.max)
            self.vdr()
            self.cp(self.ws16[:, ti:ti + 1], self.zc1[:])
            self.recip(self.zc1[:], self.zc1[:])
            self.vdr()
            self.ts(wft, wft, self.zc1[:, 0:1], None, OP.mult)
            self.vdr()
            self.emit_round(wft, aq, rres, tb, sg_t)
            wv = self.inc("vector", "v_wq", 1)
            self.wait("tensor", "v_wq", wv)
            for c in range(DC):
                qn = f"d_wqt{c % 2}"
                self.wait("scalar", qn, self.val(qn))
                ev = self.tr_evac([wft[:, c * 128:(c + 1) * 128]], self.wqbf[c % 2][:])
                self.wait("scalar", "s_tre", ev)
                s = self.sem(qn)
                self.semn[qn] += 16
                self.E("scalar", lambda e, c=c, t=ti, s=s, buf=self.wqbf[c % 2]:
                       e.dma_start(out=self.wqts.ap()[c, t, :, :], in_=buf[:]).then_inc(s, 16))

    def use_state(self, p):
        self.St = self.StP[p % 3]
        self.Mi8 = self.Mi8P[p % 3]

    def emit_phase_a(self):
        self.plan = {e: [] for e in self.ENGS}
        with ExitStack() as stA:
            self.alloc_a(stA)
            self.emit_consts()
            for h in range(self.HPC):
                self.emit_build_head(h)
                self.emit_head_stats(h)
                self.emit_wquant(h)
                self.E("scalar", lambda e: e.preload_act_table(AF.Exp), kind="other")
                par0 = h * self.NG
                self.use_state(par0)
                self.emit_group_stats(h, 0)
                if self.NG > 1:
                    self.use_state(par0 + 1)
                    self.emit_group_stats(h, 1)
                self.use_state(par0)
                for gi in range(self.G):
                    self.emit_qk_slot(h, 0, gi)
                for g in range(self.NG):
                    self.use_state(par0 + g)
                    self.emit_group_counts(h, g)
                    if g + 2 < self.NG:
                        self.use_state(par0 + g + 2)
                        self.emit_group_stats(h, g + 2)
                    half = self.G // 2
                    for gi in range(self.G):
                        self.use_state(par0 + g)
                        self.emit_tile_finish(h, g, gi)
                        self.emit_tile_pv_mms(h, g, gi)
                        self.emit_tile_out(h, g, gi)
                        if g + 1 < self.NG and gi % half == half - 1:
                            self.use_state(par0 + g + 1)
                            for gj in range(gi - half + 1, gi + 1):
                                self.emit_qk_slot(h, g + 1, gj)
                self.E("scalar", lambda e: e.preload_act_table(AF.Sqrt), kind="other")
            self.barrier("a")
            self.run_block()

    def emit_phase_b(self):
        self.plan = {e: [] for e in self.ENGS}
        D, DC, S = self.D, self.DC, self.S
        with ExitStack() as stB:
            wosb = self.sbuf("wosb", [128, DC, D], F32, stB)
            agt = self.sbuf("agt", [128, DC, 512], F32, stB)
            xrs = self.sbuf("xrs", [128, D], F32, stB)
            self.wait("gpsimd", "d_ag0", self.val("d_ag0"))
            self.wait("gpsimd", "d_ag1", self.val("d_ag1"))
            s = self.sem("cc1")
            self.semn["cc1"] += 1
            self.E("gpsimd", lambda e, s=s:
                   e.collective_compute("AllGather", OP.bypass,
                                        replica_groups=[[0, 1, 2, 3], [4, 5, 6, 7]],
                                        ins=[self.ag_in.ap()], outs=[self.ag_out.ap()])
                   .then_inc(s, 1), kind="dma")
            for c in range(DC):
                self.dma("sync", wosb[:, c, :], self.wod[c * 128:(c + 1) * 128, :], "d_wo")
            self.wait("sync", "cc1", self.val("cc1"))
            condbox = {}

            def getcond(e, j):
                if "hc" not in condbox:
                    condbox["hc"] = e.snap(e.partition_id() % 4)
                return condbox["hc"] == j

            TTB = self.TTB
            nhalf = max(1, (self.TPC + 511) // 512)
            tiles_per_half = max(1, 512 // 128)
            for half in range(nhalf):
                w = min(512, self.TPC - half * 512)
                self.wait("sync", "t_wo", self.val("t_wo"))
                ag3 = self.ag_out.ap().rearrange("(c p) n -> p c n", p=128)
                for j in range(4):
                    src = ag3[:, :, j * self.TPC + half * 512:
                              j * self.TPC + half * 512 + w]
                    self.dma("sync", agt[:, :, 0:w], src, "d_agt",
                             cond=(lambda e, j=j: getcond(e, j)))
                va = self.val("d_agt")
                self.wait("tensor", "d_agt", va)
                self.wait("tensor", "d_wo", self.val("d_wo"))
                for tl in range(tiles_per_half):
                    if half * 512 + tl * 128 >= self.TPC:
                        break
                    tt_ = half * tiles_per_half + tl
                    if tt_ >= 1:
                        self.wait("tensor", "v_x1", self.val("v_x1"))
                    for c in range(DC):
                        for oq in range(D // 512):
                            self.mm(self.scps[:, oq * 512:(oq + 1) * 512],
                                    agt[:, c, tl * 128:(tl + 1) * 128],
                                    wosb[:, c, oq * 512:(oq + 1) * 512],
                                    start=(c == 0), stop=(c == DC - 1))
                    tv = self.inc("tensor", "t_wo", 1)
                    self.wait("sync", "v_x1", self.val("v_x1"))
                    xv = self.dma("sync", xrs[:],
                                  self.xres[tt_ * 128:(tt_ + 1) * 128, :], "d_xr")
                    self.wait("vector", "t_wo", tv)
                    self.wait("vector", "d_xr", xv)
                    self.wait("vector", "d_x1", self.val("d_x1"))
                    self.wait("scalar", "t_wo", tv)
                    self.emit_ln(self.scps[:, 0:D])
                    self.tt(self.y2[:], self.y2[:], xrs[:], OP.add)
                    x1v = self.inc("vector", "v_x1", 1)
                    self.vdr()
                    self.wait("sync", "v_x1", x1v)
                    self.dma("sync", self.x1s.ap()[tt_ * 128:(tt_ + 1) * 128, :],
                             self.y2[:], "d_x1")
            self.wait("sync", "d_x1", self.val("d_x1"))
            self.barrier("b")
            self.run_block()

    def emit_phase_c(self):
        self.plan = {e: [] for e in self.ENGS}
        D, DC = self.D, self.DC
        with ExitStack() as stC:
            wqtsb = self.sbuf("wqtsb", [128, DC, D], BF16, stC)
            wst = self.sbuf("wst", [128, D], F32, stC)
            btt = self.sbuf("btt", [128, D], F32, stC)
            x1t = self.sbuf("x1t", [128, D], F32, stC)
            q32 = self.sbuf("q32", [128, D], F32, stC)
            aq2 = self.sbuf("aq2", [128, D], F32, stC)
            rr2 = self.sbuf("rr2", [128, D], F32, stC)
            sgn2 = self.sbuf("sgn2", [128, D], F32, stC)
            tb2 = self.sbuf("tb2", [128, D], F32, stC)
            xqbf = self.sbuf("xqbf", [128, DC, 128], BF16, stC)
            onesr = self.sbuf("onesr", [1, 128], F32, stC)
            wsrow = self.sbuf("wsrow", [1, D], F32, stC)
            bfrow = self.sbuf("bfrow", [1, D], F32, stC)
            self.E("scalar", lambda e: e.preload_act_table(AF.Sqrt), kind="other")
            self.wait("sync", "d_wqt0", self.val("d_wqt0"))
            self.wait("sync", "d_wqt1", self.val("d_wqt1"))
            for c in range(DC):
                src = self.wqts.ap()[c].rearrange("o p n -> p o n")
                self.dma("sync", wqtsb[:, c, :].rearrange("p (o n) -> p o n", n=128),
                         src, "d_wqL")
            self.memset(onesr[:], 1.0)
            ov = self.inc("vector", "v_one", 1)
            bank = self.trps[0]
            self.wait("tensor", "s_tre", self.val("s_tre"))
            self.E("tensor", lambda e, b=bank, idt=self.identt:
                   e.transpose(b[0:DC, 0:128], self.ws16[:], idt[:]))
            tv = self.inc("tensor", "t_tra", 1)
            self.wait("scalar", "t_tra", tv)
            self.E("scalar", lambda e, b=bank, DC=DC:
                   e.activation(out=self.wTsb[0][0:DC, 0:128], in_=b[0:DC, 0:128],
                                func=AF.Copy))
            ev = self.inc("scalar", "s_tre", 1)
            self.wait("sync", "s_tre", ev)
            vd = self.dma("sync", wsrow[0:1, :],
                          self.wTsb[0][0:DC, 0:128], "d_wsr")
            bv = self.dma("sync", bfrow[:], self.bfd, "d_bfr")
            self.wait("tensor", "d_wsr", vd)
            self.wait("tensor", "d_bfr", bv)
            self.wait("tensor", "v_one", ov)
            for oq in range(D // 512):
                self.mm(self.scps[:, 0:512] if D == 512 else self.scps[:, oq * 512:(oq + 1) * 512],
                        onesr[:], wsrow[:, oq * 512:(oq + 1) * 512])
            tv = self.inc("tensor", "t_bc", 1)
            self.wait("scalar", "t_bc", tv)
            self.E("scalar", lambda e, D=D: e.activation(out=wst[:], in_=self.scps[:, 0:D],
                                                         func=AF.Copy))
            sv = self.inc("scalar", "s_bc", 1)
            self.wait("tensor", "s_bc", sv)
            for oq in range(D // 512):
                self.mm(self.scps[:, oq * 512:(oq + 1) * 512],
                        onesr[:], bfrow[:, oq * 512:(oq + 1) * 512])
            tv = self.inc("tensor", "t_bc", 1)
            self.wait("scalar", "t_bc", tv)
            self.E("scalar", lambda e, D=D: e.activation(out=btt[:], in_=self.scps[:, 0:D],
                                                         func=AF.Copy))
            sv = self.inc("scalar", "s_bc", 1)
            self.wait("vector", "s_bc", sv)
            for tt_ in range(self.TTB):
                self.wait("sync", "v_out", self.val("v_out"))
                xv = self.dma("sync", x1t[:], self.x1s.ap()[tt_ * 128:(tt_ + 1) * 128, :],
                              "d_x1r")
                self.wait("vector", "d_x1r", xv)
                self.wait("vector", "d_out", self.val("d_out"))
                self.emit_ln(x1t[:])
                self.rred(self.zc1[:], self.y2[:], OP.max, absval=True)
                self.vdr()
                self.ts(self.xsc[:], self.zc1[:], 1.0 / 127.0, 1e-12, OP.mult, OP.max)
                self.vdr()
                self.recip(self.zc1[:], self.xsc[:])
                self.vdr()
                self.ts(q32[:], self.y2[:], self.zc1[:, 0:1], None, OP.mult)
                self.vdr()
                self.emit_round(q32[:], aq2[:], rr2[:], tb2[:], sgn2[:])
                xqv = self.inc("vector", "v_xq", 1)
                self.wait("tensor", "v_xq", xqv)
                for c in range(DC):
                    self.tr_evac([q32[:, c * 128:(c + 1) * 128]], xqbf[:, c, :])
                self.wait("tensor", "s_tre", self.val("s_tre"))
                self.wait("tensor", "d_wqL", self.val("d_wqL"))
                if tt_ >= 1:
                    self.wait("tensor", "v_ffo", self.val("v_ffo"))
                for c in range(DC):
                    for oq in range(D // 512):
                        self.mm(self.scps[:, oq * 512:(oq + 1) * 512],
                                xqbf[:, c, :], wqtsb[:, c, oq * 512:(oq + 1) * 512],
                                start=(c == 0), stop=(c == DC - 1))
                fv = self.inc("tensor", "t_ffn", 1)
                self.wait("vector", "t_ffn", fv)
                self.ts(self.y2[:], self.scps[:, 0:D], self.xsc[:, 0:1], None, OP.mult)
                self.vdr()
                self.tt(self.y2[:], self.y2[:], wst[:], OP.mult)
                self.vdr()
                self.tt(self.y2[:], self.y2[:], btt[:], OP.add)
                self.vdr()
                self.cp(q32[:], self.y2[:])
                yv = self.inc("vector", "v_ffo", 1)
                self.vdr()
                self.emit_ln(q32[:])
                self.tt(self.y2[:], self.y2[:], x1t[:], OP.add)
                ov2 = self.inc("vector", "v_out", 1)
                self.vdr()
                self.wait("sync", "v_out", ov2)
                self.dma("sync", self.outd[tt_ * 128:(tt_ + 1) * 128, :], self.y2[:], "d_out")
            for e in self.ENGS:
                self.wait(e, "d_out", self.val("d_out"))
            self.run_block()

    def build(self):
        self.declare_io()
        self.alloc_shared()
        self.emit_phase_a()
        self.emit_phase_b()
        self.emit_phase_c()
        self.es.close()
        return self.nc


def _host_constants(cfg):
    W = cfg["WIN"]
    BANDW = 2 * W + 128
    ident = np.eye(128, dtype=np.float32)
    r = np.arange(128)[:, None]
    c = np.arange(BANDW)[None, :]
    band = ((c >= r) & (c <= r + 2 * W)).astype(np.uint8)
    iota16 = np.broadcast_to(np.arange(16, dtype=np.float32)[None, :], (128, 16)).copy()
    return ident, band, iota16


def make_in_maps(cfg, x, Wq, Wk, Wv, Wo, Wf, bfv):
    ident, band, iota16 = _host_constants(cfg)
    HSL = cfg["HPC"] * 128
    TPC = cfg["S"] // 4
    in_maps = []
    for core in range(8):
        b = core // 4
        hc = core % 4
        in_maps.append(dict(
            xb=np.ascontiguousarray(x[b]),
            xres=np.ascontiguousarray(x[b][hc * TPC:(hc + 1) * TPC]),
            wq=np.ascontiguousarray(Wq[:, hc * HSL:(hc + 1) * HSL]),
            wk=np.ascontiguousarray(Wk[:, hc * HSL:(hc + 1) * HSL]),
            wv=np.ascontiguousarray(Wv[:, hc * HSL:(hc + 1) * HSL]),
            wo=Wo, wf=Wf, bf=bfv,
            ident=ident, bandm=band, iota16=iota16,
        ))
    return in_maps


_BUILD_CACHE = {}


def kernel(L0_attn_hidden_states, L0_attn_Wq, L0_attn_Wk, L0_attn_Wv, L0_attn_Wo,
           L0_ffn_weight, L0_ffn_bias):
    cfg = default_cfg()
    x = np.ascontiguousarray(np.asarray(L0_attn_hidden_states, dtype=np.float32))
    Wq = np.ascontiguousarray(np.asarray(L0_attn_Wq, dtype=np.float32))
    Wk = np.ascontiguousarray(np.asarray(L0_attn_Wk, dtype=np.float32))
    Wv = np.ascontiguousarray(np.asarray(L0_attn_Wv, dtype=np.float32))
    Wo = np.ascontiguousarray(np.asarray(L0_attn_Wo, dtype=np.float32))
    Wf = np.ascontiguousarray(np.asarray(L0_ffn_weight, dtype=np.float32))
    bfv = np.ascontiguousarray(np.asarray(L0_ffn_bias, dtype=np.float32)).reshape(1, -1)
    if "full" not in _BUILD_CACHE:
        _BUILD_CACHE["full"] = MK(cfg).build()
    nc = _BUILD_CACHE["full"]
    in_maps = make_in_maps(cfg, x, Wq, Wk, Wv, Wo, Wf, bfv)
    res = run_bass_kernel_spmd(nc, in_maps, core_ids=list(range(8)))
    S, D = cfg["S"], cfg["D"]
    TPC = S // 4
    out = np.empty((2, S, D), dtype=np.float32)
    for core in range(8):
        b = core // 4
        hc = core % 4
        out[b, hc * TPC:(hc + 1) * TPC] = res.results[core]["out"]
    return out
